# version 11
# baseline (speedup 1.0000x reference)
import math
from contextlib import ExitStack

import numpy as np
import ml_dtypes

import concourse.bass as bass
import concourse.mybir as mybir
from concourse.bass_utils import run_bass_kernel_spmd

F32 = mybir.dt.float32
BF16 = mybir.dt.bfloat16
AF = mybir.ActivationFunctionType
ALU = mybir.AluOpType
BFNP = ml_dtypes.bfloat16

B, T, DM = 2, 16384, 1024
DI, DS, DCONV = 2048, 16, 4
TQ = T // 4
CG = DI // 4
EPS_RMS = 1e-6
EPS_LN = 1e-5
N_CORES = 8


class Sem:

    def __init__(self, es, nc, name):
        self.h = es.enter_context(nc.semaphore(name))
        self.n = 0

    def inc(self, instr, k=1):
        instr.then_inc(self.h, k)
        self.n += k
        return self.n


def _sb(es, nc, name, shape, dtype):
    return es.enter_context(nc.sbuf_tensor(name, shape, dtype))


def _ps(es, nc, name, shape, dtype=F32):
    return es.enter_context(nc.psum_tensor(name, shape, dtype))


class Arena:

    def __init__(self, es, nc, name, words):
        self.t = es.enter_context(nc.sbuf_tensor(name, [128, words], F32))

    def view(self, off, shape, dtype):
        elems = 1
        for s in shape[1:]:
            elems *= s
        words = elems * (2 if dtype == BF16 else 4) // 4
        ap = self.t[:, off:off + words]
        if dtype == BF16:
            ap = ap.bitcast(BF16)
        if len(shape) == 2:
            return ap
        if len(shape) == 3:
            return ap.rearrange("p (a b) -> p a b", a=shape[1])
        if len(shape) == 4:
            return ap.rearrange("p (a b c) -> p a b c", a=shape[1], b=shape[2])
        raise ValueError(shape)


def build_l1():
    nc = bass.Bass("TRN2", target_bir_lowering=False)
    TC = 512
    NCH = TQ // TC

    xq_d = nc.dram_tensor("xq", [DM, TQ], F32, kind="ExternalInput")
    xh_d = nc.dram_tensor("xh", [DM, 3], F32, kind="ExternalInput")
    winT_d = nc.dram_tensor("winT", [DM, 2 * DI], F32, kind="ExternalInput")
    wxT_d = nc.dram_tensor("wxT", [DI, 33], F32, kind="ExternalInput")
    g_d = nc.dram_tensor("gvec", [128, 8], F32, kind="ExternalInput")
    cdiag_d = nc.dram_tensor("cdiag", [128, 64 * 128], F32, kind="ExternalInput")
    cb_d = nc.dram_tensor("cb", [128, 16], F32, kind="ExternalInput")
    ident_d = nc.dram_tensor("ident", [128, 128], F32, kind="ExternalInput")

    xc_d = nc.dram_tensor("xc", [DI, TQ], BF16, kind="ExternalOutput")
    z_d = nc.dram_tensor("z", [DI, TQ], BF16, kind="ExternalOutput")
    par_d = nc.dram_tensor("par", [33, TQ], F32, kind="ExternalOutput")

    es = ExitStack()
    with es:
        w_sb = _sb(es, nc, "w_sb", [128, 8, 2 * DI], BF16)
        wstg = _sb(es, nc, "wstg", [128, 8, 512], F32)
        wx_sb = _sb(es, nc, "wx_sb", [128, 16, 33], BF16)
        wxstg = _sb(es, nc, "wxstg", [128, 16, 33], F32)
        cd_sb = _sb(es, nc, "cd_sb", [128, 64, 128], BF16)
        cb_sb = _sb(es, nc, "cb_sb", [128, 16], F32)
        g_sb = _sb(es, nc, "g_sb", [128, 8], F32)
        id_sb = _sb(es, nc, "id_sb", [128, 128], F32)
        ones_sb = _sb(es, nc, "ones_sb", [1, 128], F32)
        onesc_sb = _sb(es, nc, "onesc_sb", [128, 1], F32)
        sx = _sb(es, nc, "sx", [128, 8, TC], F32)
        sxg = _sb(es, nc, "sxg", [128, 8, TC], BF16)
        ssq = _sb(es, nc, "ssq", [128, 8, TC], F32)
        rowa = _sb(es, nc, "rowa", [1, TC], F32)
        rowb = _sb(es, nc, "rowb", [1, TC], F32)
        rstd_rep = _sb(es, nc, "rstd_rep", [128, TC], F32)
        xs_buf = _sb(es, nc, "xs_buf", [128, 16, TC + 3], BF16)
        z_sb = _sb(es, nc, "z_sb", [128, 16, TC], BF16)
        xc_sb = _sb(es, nc, "xc_sb", [128, 16, TC], BF16)
        par_tm = _sb(es, nc, "par_tm", [128, 2, 33], F32)
        sxh = _sb(es, nc, "sxh", [128, 8, 3], F32)
        sxgh = _sb(es, nc, "sxgh", [128, 8, 3], BF16)
        sqh = _sb(es, nc, "sqh", [128, 8, 3], F32)
        rsh = _sb(es, nc, "rsh", [128, 3], F32)
        rowh = _sb(es, nc, "rowh", [1, 3], F32)
        par_fm = _sb(es, nc, "par_fm", [33, TC], F32)

        ps_ss = _ps(es, nc, "ps_ss", [1, TC])
        ps_rep = _ps(es, nc, "ps_rep", [128, TC])
        ps_xz = [_ps(es, nc, f"ps_xz{i}", [128, TC]) for i in range(2)]
        ps_cv = [_ps(es, nc, f"ps_cv{i}", [128, TC]) for i in range(2)]
        ps_par = _ps(es, nc, "ps_par", [128, 33])
        ps_tr = _ps(es, nc, "ps_tr", [33, 128])

        ld_w = Sem(es, nc, "ld_w")
        ld_x = Sem(es, nc, "ld_x")
        a_wc = Sem(es, nc, "a_wc")
        a_xg = Sem(es, nc, "a_xg")
        p_ss = Sem(es, nc, "p_ss")
        d_row = Sem(es, nc, "d_row")
        a_row = Sem(es, nc, "a_row")
        p_rep = Sem(es, nc, "p_rep")
        a_rep = Sem(es, nc, "a_rep")
        p_xz = Sem(es, nc, "p_xz")
        d_xz = Sem(es, nc, "d_xz")
        p_cv = Sem(es, nc, "p_cv")
        a_cv = Sem(es, nc, "a_cv")
        p_par = Sem(es, nc, "p_par")
        a_par = Sem(es, nc, "a_par")
        p_tr = Sem(es, nc, "p_tr")
        a_tr = Sem(es, nc, "a_tr")
        g_out = Sem(es, nc, "g_out")
        halo = Sem(es, nc, "halo")
        ld_h = Sem(es, nc, "ld_h")

        block = es.enter_context(nc.Block())

        @block.sync
        def _(sync):
            for sl in range(8):
                sync.dma_start(
                    out=wstg[:, :, :],
                    in_=winT_d[:, :].rearrange("(c p) o -> p c o", p=128)[
                        :, :, sl * 512:(sl + 1) * 512],
                ).then_inc(ld_w.h, 16)
                ld_w.n += 16
                sync.wait_ge(a_wc.h, sl + 1)
            for h in range(2):
                sync.dma_start(
                    out=wstg[:, :, :],
                    in_=cdiag_d[:, h * 4096:(h + 1) * 4096].rearrange(
                        "p (n o) -> p n o", o=512),
                ).then_inc(ld_w.h, 16)
                ld_w.n += 16
                sync.wait_ge(a_wc.h, 9 + h)
            for src, dst in ((wxT_d[:, :].rearrange("(c p) o -> p c o", p=128),
                              wxstg[:, :, :]),
                             (cb_d[:, :], cb_sb[:]),
                             (g_d[:, :], g_sb[:]),
                             (ident_d[:, :], id_sb[:])):
                sync.dma_start(out=dst, in_=src).then_inc(ld_w.h, 16)
                ld_w.n += 16
            sync.dma_start(
                out=sxh[:, :, :],
                in_=xh_d[:, :].rearrange("(c p) t -> p c t", p=128),
            ).then_inc(ld_h.h, 16)
            ld_h.n += 16
            for j in range(NCH):
                sync.dma_start(
                    out=sx[:, :, :],
                    in_=xq_d[:, :].rearrange("(c p) t -> p c t", p=128)[
                        :, :, j * TC:(j + 1) * TC],
                ).then_inc(ld_x.h, 16)
                ld_x.n += 16
                if j + 1 < NCH:
                    sync.wait_ge(a_xg.h, 2 * (j + 1))

        @block.vector
        def _(vector):
            vector.memset(ones_sb[:], 1.0)
            v = vector.memset(onesc_sb[:], 1.0)
            halo.inc(v)
            vector.wait_ge(halo.h, 3)
            v = vector.tensor_scalar(
                out=rowh[:], in0=ps_ss[0:1, 0:3], scalar1=1.0 / DM,
                scalar2=EPS_RMS, op0=ALU.mult, op1=ALU.add)
            halo.inc(v)
            for oc in range(16):
                vector.wait_ge(halo.h, 8 + 2 * oc)
                v = vector.tensor_tensor(out=xs_buf[:, oc, 0:3],
                                         in0=ps_xz[oc % 2][0:128, 0:3],
                                         in1=rsh[:], op=ALU.mult)
                halo.inc(v)
            for j in range(NCH):
                vector.wait_ge(p_ss.h, j + 1)
                if j > 0:
                    vector.wait_ge(p_rep.h, j)
                v = vector.tensor_scalar(
                    out=rowa[:], in0=ps_ss[:], scalar1=1.0 / DM,
                    scalar2=EPS_RMS, op0=ALU.mult, op1=ALU.add)
                d_row.inc(v)
                if j > 0:
                    vector.wait_ge(p_cv.h, 16 * j)
                    for dc in range(16):
                        v = vector.tensor_copy(out=xs_buf[:, dc, 0:3],
                                               in_=xs_buf[:, dc, TC:TC + 3])
                        d_xz.inc(v)
                else:
                    for dc in range(16):
                        v = vector.engine_nop()
                        d_xz.inc(v)
                vector.wait_ge(a_rep.h, j + 1)
                if j > 0:
                    vector.wait_ge(g_out.h, 48 * (j - 1) + 32)
                for oc in range(32):
                    vector.wait_ge(p_xz.h, 32 * j + oc + 1)
                    dst = (xs_buf[:, oc, 3:3 + TC] if oc < 16
                           else z_sb[:, oc - 16, :])
                    v = vector.tensor_tensor(out=dst, in0=ps_xz[oc % 2][:],
                                             in1=rstd_rep[:], op=ALU.mult)
                    d_xz.inc(v)

        @block.scalar
        def _(scalar):
            for sl in range(8):
                scalar.wait_ge(ld_w.h, 16 * (sl + 1))
                v = scalar.copy(out=w_sb[:, :, sl * 512:(sl + 1) * 512],
                                in_=wstg[:, :, :])
                a_wc.inc(v)
            for h in range(2):
                scalar.wait_ge(ld_w.h, 16 * (9 + h))
                v = scalar.copy(
                    out=cd_sb[:, h * 32:(h + 1) * 32, :].rearrange(
                        "p n o -> p (n o)").rearrange("p (n o) -> p n o", o=512),
                    in_=wstg[:, :, :])
                a_wc.inc(v)
            scalar.wait_ge(ld_w.h, 14 * 16)
            scalar.copy(out=wx_sb[:, :, :], in_=wxstg[:, :, :])
            scalar.wait_ge(ld_h.h, 16)
            scalar.wait_ge(ld_w.h, 13 * 16)
            for c in range(8):
                scalar.activation(out=sxgh[:, c, :], in_=sxh[:, c, :],
                                  func=AF.Copy, scale=g_sb[:, c:c + 1])
            for c in range(8):
                v = scalar.activation(out=sqh[:, c, :], in_=sxh[:, c, :],
                                      func=AF.Square)
            halo.inc(v)
            scalar.wait_ge(halo.h, 4)
            scalar.activation(out=rowb[0:1, 0:3], in_=rowh[:], func=AF.Ln)
            v = scalar.activation(out=rowh[:], in_=rowb[0:1, 0:3],
                                  func=AF.Exp, scale=-0.5)
            halo.inc(v)
            scalar.wait_ge(halo.h, 6)
            v = scalar.copy(out=rsh[:], in_=ps_rep[0:128, 0:3])
            halo.inc(v)
            for j in range(NCH):
                scalar.wait_ge(ld_x.h, 16 * (j + 1))
                if j > 0:
                    scalar.wait_ge(p_xz.h, 32 * j)
                for c in range(8):
                    v = scalar.activation(out=sxg[:, c, :], in_=sx[:, c, :],
                                          func=AF.Copy, scale=g_sb[:, c:c + 1])
                a_xg.inc(v)
                if j > 0:
                    scalar.wait_ge(p_ss.h, j)
                for c in range(8):
                    v = scalar.activation(out=ssq[:, c, :], in_=sx[:, c, :],
                                          func=AF.Square)
                a_xg.inc(v)
                scalar.wait_ge(d_row.h, j + 1)
                scalar.activation(out=rowb[:], in_=rowa[:], func=AF.Ln)
                v = scalar.activation(out=rowa[:], in_=rowb[:], func=AF.Exp,
                                      scale=-0.5)
                a_row.inc(v)
                scalar.wait_ge(p_rep.h, j + 1)
                if j > 0:
                    scalar.wait_ge(d_xz.h, 48 * j)
                v = scalar.copy(out=rstd_rep[:], in_=ps_rep[:])
                a_rep.inc(v)
                if j > 0:
                    scalar.wait_ge(p_par.h, 4 * j)
                    scalar.wait_ge(g_out.h, 48 * (j - 1) + 16)
                for dc in range(16):
                    scalar.wait_ge(p_cv.h, 16 * j + dc + 1)
                    v = scalar.activation(out=xc_sb[:, dc, :],
                                          in_=ps_cv[dc % 2][:], func=AF.Silu,
                                          bias=cb_sb[:, dc:dc + 1])
                    a_cv.inc(v)
                if j > 0:
                    scalar.wait_ge(g_out.h, 48 * j)
                for tt in range(4):
                    scalar.wait_ge(p_par.h, 4 * j + tt + 1)
                    v = scalar.copy(out=par_tm[:, tt % 2, :], in_=ps_par[:])
                    a_par.inc(v)
                    scalar.wait_ge(p_tr.h, 4 * j + tt + 1)
                    v = scalar.copy(out=par_fm[:, tt * 128:(tt + 1) * 128],
                                    in_=ps_tr[:])
                    a_tr.inc(v)

        @block.tensor
        def _(tensor):
            tensor.wait_ge(halo.h, 2)
            for c in range(8):
                v = tensor.matmul(ps_ss[0:1, 0:3], onesc_sb[:], sqh[:, c, :],
                                  start=(c == 0), stop=(c == 7))
            halo.inc(v)
            tensor.wait_ge(halo.h, 5)
            v = tensor.matmul(ps_rep[0:128, 0:3], ones_sb[:], rowh[:],
                              start=True, stop=True)
            halo.inc(v)
            tensor.wait_ge(halo.h, 7)
            for oc in range(16):
                if oc >= 2:
                    tensor.wait_ge(halo.h, 9 + 2 * (oc - 2))
                for c in range(8):
                    v = tensor.matmul(ps_xz[oc % 2][0:128, 0:3],
                                      w_sb[:, c, oc * 128:(oc + 1) * 128],
                                      sxgh[:, c, :],
                                      start=(c == 0), stop=(c == 7))
                halo.inc(v)
            tensor.wait_ge(halo.h, 39)
            for j in range(NCH):
                tensor.wait_ge(a_xg.h, 2 * (j + 1))
                if j > 0:
                    tensor.wait_ge(d_row.h, j)
                for c in range(8):
                    v = tensor.matmul(ps_ss[:], onesc_sb[:], ssq[:, c, :],
                                      start=(c == 0), stop=(c == 7))
                p_ss.inc(v)
                tensor.wait_ge(a_row.h, j + 1)
                if j > 0:
                    tensor.wait_ge(a_rep.h, j)
                v = tensor.matmul(ps_rep[:], ones_sb[:], rowa[:],
                                  start=True, stop=True)
                p_rep.inc(v)
                for oc in range(32):
                    if j > 0 or oc >= 2:
                        if oc >= 2:
                            need = 48 * j + 16 + (oc - 2) + 1
                        else:
                            need = 48 * (j - 1) + 16 + (30 + oc) + 1
                        tensor.wait_ge(d_xz.h, need)
                    for c in range(8):
                        v = tensor.matmul(
                            ps_xz[oc % 2][:],
                            w_sb[:, c, oc * 128:(oc + 1) * 128],
                            sxg[:, c, :], start=(c == 0), stop=(c == 7))
                    p_xz.inc(v)
                for dc in range(16):
                    tensor.wait_ge(d_xz.h, 48 * j + 16 + dc + 1)
                    if j > 0 or dc >= 2:
                        tensor.wait_ge(a_cv.h, 16 * j + dc - 1)
                    for k in range(4):
                        v = tensor.matmul(ps_cv[dc % 2][:],
                                          cd_sb[:, 4 * dc + k, :],
                                          xs_buf[:, dc, k:k + TC],
                                          start=(k == 0), stop=(k == 3))
                    p_cv.inc(v)
                tensor.wait_ge(a_cv.h, 16 * (j + 1))
                for tt in range(4):
                    if j > 0 or tt > 0:
                        tensor.wait_ge(a_par.h, 4 * j + tt)
                    for dc in range(16):
                        v = tensor.matmul(
                            ps_par[:], xc_sb[:, dc, tt * 128:(tt + 1) * 128],
                            wx_sb[:, dc, :], start=(dc == 0), stop=(dc == 15))
                    p_par.inc(v)
                    tensor.wait_ge(a_par.h, 4 * j + tt + 1)
                    if j > 0 or tt > 0:
                        tensor.wait_ge(a_tr.h, 4 * j + tt)
                    v = tensor.transpose(ps_tr[:], par_tm[:, tt % 2, :],
                                         id_sb[:])
                    p_tr.inc(v)

        @block.gpsimd
        def _(g):
            for j in range(NCH):
                g.wait_ge(a_cv.h, 16 * (j + 1))
                v = g.dma_start(
                    out=xc_d[:, :].rearrange("(c p) t -> p c t", p=128)[
                        :, :, j * TC:(j + 1) * TC],
                    in_=xc_sb[:, :, :])
                g_out.inc(v, 16)
                g.wait_ge(d_xz.h, 48 * (j + 1))
                v = g.dma_start(
                    out=z_d[:, :].rearrange("(c p) t -> p c t", p=128)[
                        :, :, j * TC:(j + 1) * TC],
                    in_=z_sb[:, :, :])
                g_out.inc(v, 16)
                g.wait_ge(a_tr.h, 4 * (j + 1))
                v = g.dma_start(out=par_d[:, j * TC:(j + 1) * TC],
                                in_=par_fm[:, :])
                g_out.inc(v, 16)
            g.wait_ge(g_out.h, g_out.n)

    return nc


def build_l2():
    nc = bass.Bass("TRN2", target_bir_lowering=False)
    TC = 1024
    TB = TC + 2
    BL = 4 * TB
    FD4 = 4 * TB - 2
    NCH = T // TC

    xc_d = nc.dram_tensor("xc", [CG, T], BF16, kind="ExternalInput")
    z_d = nc.dram_tensor("z", [CG, T], BF16, kind="ExternalInput")
    par_d = nc.dram_tensor("par", [33, T], F32, kind="ExternalInput")
    alog_d = nc.dram_tensor("alog", [128, 4 * DS], F32, kind="ExternalInput")
    wdt_d = nc.dram_tensor("wdt", [1, CG], F32, kind="ExternalInput")
    bdt_d = nc.dram_tensor("bdt", [128, 4], F32, kind="ExternalInput")
    dvec_d = nc.dram_tensor("dvec", [128, 4], F32, kind="ExternalInput")
    sel_d = nc.dram_tensor("sel", [32, 32 * 128], F32, kind="ExternalInput")

    g_d = nc.dram_tensor("gact", [CG, T], BF16, kind="ExternalOutput")

    es = ExitStack()
    with es:
        xc_sb = _sb(es, nc, "xc_sb", [128, 2, 4, TC], BF16)
        z_sb = _sb(es, nc, "z_sb", [128, 4, TC], BF16)
        bc_sb = _sb(es, nc, "bc_sb", [33, 2, TC], F32)
        bcbf_sb = _sb(es, nc, "bcbf_sb", [32, 2, TC], BF16)
        dt_sb = _sb(es, nc, "dt_sb", [128, 4, TC], BF16)
        u_sb = _sb(es, nc, "u_sb", [128, 4, TC], BF16)
        sp_r = _sb(es, nc, "sp_r", [128, TC], F32)
        t_a = _sb(es, nc, "t_a", [128, TC], F32)
        brep = _sb(es, nc, "brep", [128, 2, 4, TC], BF16)
        crep = _sb(es, nc, "crep", [128, 2, 4, TC], BF16)
        a4 = _sb(es, nc, "a4", [128, 2, BL], F32)
        bu4 = _sb(es, nc, "bu4", [128, 2, BL], BF16)
        h4 = _sb(es, nc, "h4", [128, 2, BL], BF16)
        tmp4 = _sb(es, nc, "tmp4", [128, 4, TC], BF16)
        pair2 = _sb(es, nc, "pair2", [128, 2, TC], BF16)
        yac_sb = _sb(es, nc, "yac_sb", [128, 4, TC], BF16)
        t1_sb = _sb(es, nc, "t1_sb", [128, TC], F32)
        sz_sb = _sb(es, nc, "sz_sb", [128, TC], BF16)
        gout_sb = _sb(es, nc, "gout_sb", [128, 1, 4, TC], BF16)
        carry = _sb(es, nc, "carry", [128, 4, DS], F32)
        negA = _sb(es, nc, "negA", [128, 4, DS], F32)
        ea_sb = _sb(es, nc, "ea_sb", [128, 4 * DS], F32)
        wdt_sb = _sb(es, nc, "wdt_sb", [33, CG], F32)
        bdt_sb = _sb(es, nc, "bdt_sb", [128, 4], F32)
        dv_sb = _sb(es, nc, "dv_sb", [128, 4], F32)
        ones_sb = _sb(es, nc, "ones_sb", [1, 128], F32)
        sel_sb = _sb(es, nc, "sel_sb", [32, 32, 128], BF16)

        ps_dt = _ps(es, nc, "ps_dt", [128, TC])
        ps_bc = [_ps(es, nc, f"ps_bc{i}", [128, TC]) for i in range(2)]

        ld_w = Sem(es, nc, "ld_w")
        ld_c = Sem(es, nc, "ld_c")
        d_pre = Sem(es, nc, "d_pre")
        p_dt = Sem(es, nc, "p_dt")
        a_spp = Sem(es, nc, "a_spp")
        a_spl = Sem(es, nc, "a_spl")
        a_bcc = Sem(es, nc, "a_bcc")
        d_dt = Sem(es, nc, "d_dt")
        p_bc = Sem(es, nc, "p_bc")
        a_bc = Sem(es, nc, "a_bc")
        a_ex = Sem(es, nc, "a_ex")
        g_bu = Sem(es, nc, "g_bu")
        d_sc = Sem(es, nc, "d_sc")
        d_cy = Sem(es, nc, "d_cy")
        d_cm = Sem(es, nc, "d_cm")
        d_ga = Sem(es, nc, "d_ga")
        a_sz = Sem(es, nc, "a_sz")
        g_gt = Sem(es, nc, "g_gt")
        g_dma = Sem(es, nc, "g_dma")

        selstg = a4[0:32, 0, 0:4096].rearrange("p (n o) -> p n o", o=128)

        block = es.enter_context(nc.Block())
        LDW_TOTAL = 5 * 16

        def sent_pair(buf_ap):
            return buf_ap[:, TC:TC + 3 * TB].rearrange(
                "p (b w) -> p b w", w=TB)[:, :, 0:2]

        def sent_col(buf_ap, which):
            return buf_ap[:, TC:TC + 3 * TB].rearrange(
                "p (b w) -> p b w", w=TB)[:, :, which:which + 1]

        def tiles_view(buf_ap):
            return buf_ap.rearrange("p (b w) -> p b w", w=TB)[:, :, 0:TC]

        def carry_cols(hb):
            return bass.AP(h4, hb * BL + (TC - 1), [[2 * BL, 128], [TB, 4]])

        @block.sync
        def _(sync):
            for src, dst in ((alog_d[:, :], ea_sb[:]),
                             (wdt_d[:, :], wdt_sb[32:33, :]),
                             (bdt_d[:, :], bdt_sb[:]),
                             (sel_d[:, :].rearrange("p (n o) -> p n o", o=128),
                              selstg),
                             (dvec_d[:, :], dv_sb[:])):
                sync.dma_start(out=dst, in_=src).then_inc(ld_w.h, 16)
                ld_w.n += 16
            for j in range(NCH):
                bf = j % 2
                if j >= 2:
                    sync.wait_ge(g_dma.h, 16 * (j - 1))
                sync.dma_start(
                    out=xc_sb[:, bf, :, :],
                    in_=xc_d[:, :].rearrange("(c p) t -> p c t", p=128)[
                        :, :, j * TC:(j + 1) * TC]).then_inc(ld_c.h, 16)
                sync.dma_start(out=bc_sb[:, bf, :],
                               in_=par_d[:, j * TC:(j + 1) * TC]
                               ).then_inc(ld_c.h, 16)
                if j >= 1:
                    sync.wait_ge(g_gt.h, 4 * j)
                sync.dma_start(
                    out=z_sb[:, :, :],
                    in_=z_d[:, :].rearrange("(c p) t -> p c t", p=128)[
                        :, :, j * TC:(j + 1) * TC]).then_inc(ld_c.h, 16)
                ld_c.n += 48

        @block.vector
        def _(vector):
            vector.memset(ones_sb[:], 1.0)
            vector.memset(carry[:], 0.0)
            vector.wait_ge(d_pre.h, 1)
            v = vector.tensor_scalar_mul(
                out=negA[:].rearrange("p a b -> p (a b)"), in0=ea_sb[:],
                scalar1=-1.0)
            d_pre.inc(v)
            for hb in range(2):
                vector.memset(sent_pair(a4[:, hb, :]), 0.0)
                vector.memset(sent_col(bu4[:, hb, :], 0), 0.0)
            for j in range(NCH):
                bf = j % 2
                if j > 0:
                    vector.wait_ge(a_ex.h, 16 * j)
                    vector.wait_ge(g_bu.h, 16 * j)
                for dc in range(4):
                    vector.wait_ge(a_spl.h, 4 * j + dc + 1)
                    v = vector.tensor_add(out=dt_sb[:, dc, :], in0=sp_r[:],
                                          in1=t_a[:])
                    vector.wait_ge(ld_c.h, 48 * j + 16)
                    v = vector.tensor_tensor(out=u_sb[:, dc, :],
                                             in0=dt_sb[:, dc, :],
                                             in1=xc_sb[:, bf, dc, :],
                                             op=ALU.mult)
                    d_dt.inc(v)
                def emit_bu(G2):
                    j2 = G2 // 16
                    sg2 = (G2 % 16) // 4
                    dc2 = G2 % 4
                    hb2 = G2 % 2
                    vector.wait_ge(a_bc.h, 32 * j2 + 8 * sg2 + 8)
                    vector.tensor_copy(
                        out=sent_col(bu4[:, hb2, :], 1).rearrange(
                            "p b c -> p (b c)"),
                        in_=carry[:, dc2, 4 * sg2 + 1:4 * sg2 + 4])
                    for k in range(4):
                        v2 = vector.tensor_tensor(
                            out=bu4[:, hb2, k * TB:k * TB + TC],
                            in0=u_sb[:, dc2, :],
                            in1=brep[:, sg2 % 2, k, :], op=ALU.mult)
                    g_bu.inc(v2)

                emit_bu(16 * j)
                for sg in range(4):
                    for dc in range(4):
                        G = 16 * j + 4 * sg + dc
                        hb = G % 2
                        vector.wait_ge(a_ex.h, G + 1)
                        v = vector.tensor_tensor_scan(
                            out=h4[:, hb, 0:FD4], data0=a4[:, hb, 0:FD4],
                            data1=bu4[:, hb, 0:FD4],
                            initial=carry[:, dc, 4 * sg:4 * sg + 1].opt(),
                            op0=ALU.mult, op1=ALU.add)
                        d_sc.inc(v)
                        if 4 * sg + dc < 15:
                            emit_bu(G + 1)
                        vector.wait_ge(d_sc.h, G + 1)
                        v = vector.tensor_copy(
                            out=carry[:, dc, 4 * sg:4 * sg + 4],
                            in_=carry_cols(hb))
                        d_cy.inc(v)
                        bq = sg % 2
                        for k in range(4):
                            v = vector.tensor_tensor(
                                out=tmp4[:, k, :],
                                in0=h4[:, hb, k * TB:k * TB + TC],
                                in1=crep[:, bq, k, :], op=ALU.mult)
                        d_cm.inc(v)
                        vector.tensor_tensor(
                            out=pair2[:].rearrange("p a b -> p (a b)"),
                            in0=tmp4[:, 0:2, :].rearrange("p a b -> p (a b)"),
                            in1=tmp4[:, 2:4, :].rearrange("p a b -> p (a b)"),
                            op=ALU.add)
                        if sg == 0:
                            vector.tensor_tensor(out=yac_sb[:, dc, :],
                                                 in0=pair2[:, 0, :],
                                                 in1=pair2[:, 1, :],
                                                 op=ALU.add)
                        else:
                            vector.tensor_tensor(out=tmp4[:, 0, :],
                                                 in0=pair2[:, 0, :],
                                                 in1=pair2[:, 1, :],
                                                 op=ALU.add)
                            vector.tensor_add(out=yac_sb[:, dc, :],
                                              in0=yac_sb[:, dc, :],
                                              in1=tmp4[:, 0, :])
                vector.wait_ge(a_spl.h, 4 * (j + 1))
                if j >= 1:
                    vector.wait_ge(g_dma.h, 16 * j)
                for dc in range(4):
                    v = vector.scalar_tensor_tensor(
                        out=t1_sb[:], in0=xc_sb[:, bf, dc, :],
                        scalar=dv_sb[:, dc:dc + 1], in1=yac_sb[:, dc, :],
                        op0=ALU.mult, op1=ALU.add)
                    vector.wait_ge(a_sz.h, 4 * j + dc + 1)
                    v = vector.tensor_tensor(out=gout_sb[:, 0, dc, :],
                                             in0=t1_sb[:], in1=sz_sb[:],
                                             op=ALU.mult)
                    g_gt.inc(v)

        @block.scalar
        def _(scalar):
            scalar.wait_ge(ld_w.h, LDW_TOTAL)
            scalar.copy(out=sel_sb[:, :, :], in_=selstg[:, :, :])
            v = scalar.activation(out=ea_sb[:], in_=ea_sb[:], func=AF.Exp)
            d_pre.inc(v)
            scalar.wait_ge(d_pre.h, 2)
            for j in range(NCH):
                bf = j % 2
                scalar.wait_ge(ld_c.h, 48 * j + 32)
                if j > 1:
                    scalar.wait_ge(p_bc.h, 32 * j)
                v = scalar.copy(out=bcbf_sb[:, bf, :], in_=bc_sb[0:32, bf, :])
                a_bcc.inc(v)
                scalar.wait_ge(g_gt.h, 4 * j)
                for dc in range(4):
                    if 4 * j + dc > 0:
                        scalar.wait_ge(d_dt.h, 4 * j + dc)
                    scalar.wait_ge(p_dt.h, 4 * j + dc + 1)
                    scalar.activation(out=sp_r[:], in_=ps_dt[:], func=AF.Relu,
                                      bias=bdt_sb[:, dc:dc + 1])
                    v = scalar.activation(out=t1_sb[:], in_=ps_dt[:],
                                          func=AF.Abs,
                                          bias=bdt_sb[:, dc:dc + 1])
                    a_spp.inc(v)
                    scalar.activation(out=t_a[:], in_=t1_sb[:], func=AF.Exp,
                                      scale=-1.0)
                    v = scalar.activation(out=t_a[:], in_=t_a[:], func=AF.Ln,
                                          bias=1.0)
                    a_spl.inc(v)
                for sg in range(4):
                    bq = sg % 2
                    for s2 in range(4):
                        prev = 16 * j + 4 * (sg - 1)
                        if prev > 0:
                            scalar.wait_ge(g_bu.h, prev)
                            scalar.wait_ge(d_cm.h, prev)
                        scalar.wait_ge(p_bc.h, 32 * j + 8 * sg + 2 * s2 + 1)
                        v = scalar.copy(out=brep[:, bq, s2, :],
                                        in_=ps_bc[(8 * sg + 2 * s2) % 2][:])
                        a_bc.inc(v)
                        scalar.wait_ge(p_bc.h, 32 * j + 8 * sg + 2 * s2 + 2)
                        v = scalar.copy(out=crep[:, bq, s2, :],
                                        in_=ps_bc[(8 * sg + 2 * s2 + 1) % 2][:])
                        a_bc.inc(v)
                    for dc in range(4):
                        G = 16 * j + 4 * sg + dc
                        hb = G % 2
                        scalar.wait_ge(d_dt.h, 4 * j + 4)
                        if G >= 2:
                            scalar.wait_ge(d_sc.h, G - 1)
                        for s2 in range(4):
                            s = 4 * sg + s2
                            v = scalar.activation(
                                out=a4[:, hb, s2 * TB:s2 * TB + TC],
                                in_=dt_sb[:, dc, :], func=AF.Exp,
                                scale=negA[:, dc, s:s + 1].opt())
                        a_ex.inc(v)
                scalar.wait_ge(ld_c.h, 48 * (j + 1))
                for dc in range(4):
                    scalar.wait_ge(g_gt.h, 4 * j + dc)
                    v = scalar.activation(out=sz_sb[:], in_=z_sb[:, dc, :],
                                          func=AF.Silu)
                    a_sz.inc(v)

        @block.tensor
        def _(tensor):
            for j in range(NCH):
                bf = j % 2
                tensor.wait_ge(ld_c.h, 48 * j + 32)
                for dc in range(4):
                    if 4 * j + dc > 0:
                        tensor.wait_ge(a_spp.h, 4 * j + dc)
                    for hh in range(2):
                        v = tensor.matmul(
                            ps_dt[:, hh * 512:(hh + 1) * 512],
                            wdt_sb[32:33, dc * 128:(dc + 1) * 128],
                            bc_sb[32:33, bf, hh * 512:(hh + 1) * 512],
                            start=True, stop=True)
                    p_dt.inc(v)
                tensor.wait_ge(a_bcc.h, j + 1)
                for sg in range(4):
                    for s2 in range(4):
                        s = 4 * sg + s2
                        for which in range(2):
                            k = 32 * j + 8 * sg + 2 * s2 + which
                            if k >= 2:
                                tensor.wait_ge(a_bc.h, k - 1)
                            row = s if which == 0 else 16 + s
                            for hh in range(2):
                                v = tensor.matmul(
                                    ps_bc[k % 2][:, hh * 512:(hh + 1) * 512],
                                    sel_sb[:, row, :],
                                    bcbf_sb[:, bf, hh * 512:(hh + 1) * 512],
                                    start=True, stop=True)
                            p_bc.inc(v)

        @block.gpsimd
        def _(g):
            for j in range(NCH):
                g.wait_ge(g_gt.h, 4 * (j + 1))
                v = g.dma_start(
                    out=g_d[:, :].rearrange("(c p) t -> p c t", p=128)[
                        :, :, j * TC:(j + 1) * TC],
                    in_=gout_sb[:, 0, :, :])
                g_dma.inc(v, 16)
            g.wait_ge(g_dma.h, g_dma.n)

    return nc


def build_l3():
    nc = bass.Bass("TRN2", target_bir_lowering=False)
    TC = 512
    NCH = TQ // TC
    TH = TQ // 2

    g_in = nc.dram_tensor("gall", [DI, TQ], BF16, kind="ExternalInput")
    xq_d = nc.dram_tensor("xq", [DM, TQ], F32, kind="ExternalInput")
    woT_d = nc.dram_tensor("woT", [DI, DM], F32, kind="ExternalInput")
    w1T_d = nc.dram_tensor("w1T", [DM, 4096], F32, kind="ExternalInput")
    b1_d = nc.dram_tensor("b1", [128, 32], F32, kind="ExternalInput")
    w2T_d = nc.dram_tensor("w2T", [4096, DM], F32, kind="ExternalInput")
    b2_d = nc.dram_tensor("b2", [128, 8], F32, kind="ExternalInput")

    out_d = nc.dram_tensor("out", [DM, TQ], F32, kind="ExternalOutput")
    x1f_d = nc.dram_tensor("x1f", [DM, TQ], F32)
    h1_d = nc.dram_tensor("h1", [4096, TQ], BF16)
    w2b_d = nc.dram_tensor("w2b", [4096, DM], BF16)

    es = ExitStack()
    with es:
        ar = Arena(es, nc, "ar", 49152)
        wo_sb = ar.view(0, [128, 16, DM], BF16)
        wstg = ar.view(8192, [128, 16, 256], F32)
        g_sb = ar.view(12288, [128, 2, 16, TC], BF16)
        x_sb = ar.view(20480, [128, 2, 8, TC], F32)
        mo_sb = ar.view(28672, [128, 2, 8, TC], F32)
        sqA = ar.view(36864, [128, 2, 8, TC], F32)
        x1cA = ar.view(45056, [128, 8, TC], F32)
        x1stg = ar.view(0, [128, 8, TH], F32)
        x1bh = ar.view(16384, [128, 8, TH], BF16)
        w1stg = ar.view(24576, [128, 2, 8, 128], F32)
        w1c = ar.view(26624, [128, 2, 8, 128], BF16)
        h1o = ar.view(27648, [128, 2, 4, TC], BF16)
        w2stg = ar.view(0, [128, 32, 128], F32)
        w2c = ar.view(4096, [128, 2, 32, 128], BF16)
        h1t = ar.view(12288, [128, 32, TC], BF16)
        x1cC = ar.view(20480, [128, 8, TC], F32)
        r2_sb = ar.view(24576, [128, 2, 8, TC], F32)
        sqC = ar.view(32768, [128, 2, 8, TC], F32)
        o_sb = ar.view(40960, [128, 8, TC], F32)
        rows = _sb(es, nc, "rows", [1, 4, TC], F32)
        rep1 = _sb(es, nc, "rep1", [128, TC], F32)
        rep2 = _sb(es, nc, "rep2", [128, TC], F32)
        ones_sb = _sb(es, nc, "ones_sb", [1, 128], F32)
        onesc_sb = _sb(es, nc, "onesc_sb", [128, 1], F32)
        b1_sb = _sb(es, nc, "b1_sb", [128, 32], F32)
        b2_sb = _sb(es, nc, "b2_sb", [128, 8], F32)

        ps_mm = [_ps(es, nc, f"ps_mm{i}", [128, TC]) for i in range(2)]
        ps_s1 = _ps(es, nc, "ps_s1", [1, TC])
        ps_s2 = _ps(es, nc, "ps_s2", [1, TC])
        ps_r1 = _ps(es, nc, "ps_r1", [128, TC])
        ps_r2 = _ps(es, nc, "ps_r2", [128, TC])

        ld_w = Sem(es, nc, "ld_w")
        ld_c = Sem(es, nc, "ld_c")
        a_wc = Sem(es, nc, "a_wc")
        p_mo = Sem(es, nc, "p_mo")
        d_mo = Sem(es, nc, "d_mo")
        a_sq = Sem(es, nc, "a_sq")
        p_st = Sem(es, nc, "p_st")
        d_rw = Sem(es, nc, "d_rw")
        p_rp = Sem(es, nc, "p_rp")
        a_rp = Sem(es, nc, "a_rp")
        d_x1 = Sem(es, nc, "d_x1")
        s_x1 = Sem(es, nc, "s_x1")
        ld_xh = Sem(es, nc, "ld_xh")
        a_xh = Sem(es, nc, "a_xh")
        ld_1 = Sem(es, nc, "ld_1")
        a_1c = Sem(es, nc, "a_1c")
        p_h1 = Sem(es, nc, "p_h1")
        a_h1 = Sem(es, nc, "a_h1")
        s_h1 = Sem(es, nc, "s_h1")
        ld_2s = Sem(es, nc, "ld_2s")
        a_2c = Sem(es, nc, "a_2c")
        s_2b = Sem(es, nc, "s_2b")
        ld_2 = Sem(es, nc, "ld_2")
        p_h2 = Sem(es, nc, "p_h2")
        a_h2 = Sem(es, nc, "a_h2")
        d_r2 = Sem(es, nc, "d_r2")
        a_q2 = Sem(es, nc, "a_q2")
        p_s2 = Sem(es, nc, "p_s2")
        d_w2 = Sem(es, nc, "d_w2")
        p_p2 = Sem(es, nc, "p_p2")
        a_p2 = Sem(es, nc, "a_p2")
        d_o = Sem(es, nc, "d_o")
        s_o = Sem(es, nc, "s_o")

        block = es.enter_context(nc.Block())
        A_X1 = 8 * NCH
        A_SX = 16 * NCH
        B_H1 = 4 * 64
        B_SH = 16 * 64

        @block.sync
        def _(sync):
            for sl in range(4):
                sync.dma_start(
                    out=wstg[:, :, :],
                    in_=woT_d[:, :].rearrange("(c p) o -> p c o", p=128)[
                        :, :, sl * 256:(sl + 1) * 256]).then_inc(ld_w.h, 16)
                ld_w.n += 16
                sync.wait_ge(a_wc.h, sl + 1)
            for src, dst in ((b1_d[:, :], b1_sb[:]), (b2_d[:, :], b2_sb[:])):
                sync.dma_start(out=dst, in_=src).then_inc(ld_w.h, 16)
                ld_w.n += 16
            for j in range(NCH):
                bf = j % 2
                if j >= 2:
                    sync.wait_ge(d_x1.h, 8 * (j - 1))
                sync.dma_start(
                    out=g_sb[:, bf, :, :],
                    in_=g_in[:, :].rearrange("(c p) t -> p c t", p=128)[
                        :, :, j * TC:(j + 1) * TC]).then_inc(ld_c.h, 16)
                sync.dma_start(
                    out=x_sb[:, bf, :, :],
                    in_=xq_d[:, :].rearrange("(c p) t -> p c t", p=128)[
                        :, :, j * TC:(j + 1) * TC]).then_inc(ld_c.h, 16)
                ld_c.n += 32
            sync.wait_ge(d_x1.h, A_X1)
            sync.wait_ge(s_x1.h, A_SX)
            sync.wait_ge(p_rp.h, 2 * NCH)
            for th in range(2):
                sync.dma_start(
                    out=x1stg[:, :, :],
                    in_=x1f_d[:, :].rearrange("(c p) t -> p c t", p=128)[
                        :, :, th * TH:(th + 1) * TH]).then_inc(ld_xh.h, 16)
                ld_xh.n += 16
                sync.wait_ge(a_xh.h, th + 1)
                for ok in range(32):
                    i = 32 * th + ok
                    bf = i % 2
                    if i >= 2:
                        sync.wait_ge(a_1c.h, i - 1)
                    sync.dma_start(
                        out=w1stg[:, bf, :, :],
                        in_=w1T_d[:, :].rearrange("(c p) o -> p c o", p=128)[
                            :, :, ok * 128:(ok + 1) * 128]).then_inc(ld_1.h, 16)
                    ld_1.n += 16
            sync.wait_ge(s_h1.h, B_SH)
            for oc in range(8):
                if oc >= 1:
                    sync.wait_ge(s_2b.h, 16 * oc)
                sync.dma_start(
                    out=w2stg[:, :, :],
                    in_=w2T_d[:, :].rearrange("(c p) o -> p c o", p=128)[
                        :, :, oc * 128:(oc + 1) * 128]).then_inc(ld_2s.h, 16)
                ld_2s.n += 16
            sync.wait_ge(s_2b.h, 8 * 16)
            for j in range(NCH):
                if j >= 1:
                    sync.wait_ge(d_r2.h, 8 * j)
                    sync.wait_ge(p_h2.h, 8 * j)
                sync.dma_start(
                    out=h1t[:, :, :],
                    in_=h1_d[:, :].rearrange("(c p) t -> p c t", p=128)[
                        :, :, j * TC:(j + 1) * TC]).then_inc(ld_2.h, 16)
                sync.dma_start(
                    out=x1cC[:, :, :],
                    in_=x1f_d[:, :].rearrange("(c p) t -> p c t", p=128)[
                        :, :, j * TC:(j + 1) * TC]).then_inc(ld_2.h, 16)
                for oc in range(8):
                    if 8 * j + oc >= 2:
                        sync.wait_ge(p_h2.h, 8 * j + oc - 1)
                    sync.dma_start(
                        out=w2c[:, (8 * j + oc) % 2, :, :],
                        in_=w2b_d[:, :].rearrange("(c p) o -> p c o", p=128)[
                            :, :, oc * 128:(oc + 1) * 128]).then_inc(ld_2.h, 16)
                    ld_2.n += 16
                ld_2.n += 32

        @block.scalar
        def _(scalar):
            for sl in range(4):
                scalar.wait_ge(ld_w.h, 16 * (sl + 1))
                v = scalar.copy(out=wo_sb[:, :, sl * 256:(sl + 1) * 256],
                                in_=wstg[:, :, :])
                a_wc.inc(v)
            for j in range(NCH):
                bf = j % 2
                if j >= 2:
                    scalar.wait_ge(d_x1.h, 8 * (j - 1))
                    scalar.wait_ge(p_st.h, 2 * (j - 1))
                for c in range(8):
                    scalar.wait_ge(d_mo.h, 8 * j + c + 1)
                    v = scalar.activation(out=sqA[:, bf, c, :],
                                          in_=mo_sb[:, bf, c, :],
                                          func=AF.Square)
                a_sq.inc(v)
                scalar.wait_ge(d_rw.h, 3 * j + 1)
                scalar.activation(out=rows[:, 3, :], in_=rows[:, 2, :],
                                  func=AF.Ln)
                v = scalar.activation(out=rows[:, 2, :], in_=rows[:, 3, :],
                                      func=AF.Exp, scale=-0.5)
                d_rw.inc(v)
                scalar.wait_ge(p_rp.h, 2 * (j + 1))
                v = scalar.copy(out=rep1[:], in_=ps_r1[:])
                a_rp.inc(v)
                v = scalar.copy(out=rep2[:], in_=ps_r2[:])
                a_rp.inc(v)
            for th in range(2):
                scalar.wait_ge(ld_xh.h, 16 * (th + 1))
                if th > 0:
                    scalar.wait_ge(p_h1.h, 4 * 32)
                for c in range(8):
                    v = scalar.copy(out=x1bh[:, c, :], in_=x1stg[:, c, :])
                a_xh.inc(v)
                for ok in range(32):
                    i = 32 * th + ok
                    bf = i % 2
                    scalar.wait_ge(ld_1.h, 16 * (i + 1))
                    if i >= 2:
                        scalar.wait_ge(p_h1.h, 4 * (i - 1))
                    v = scalar.copy(out=w1c[:, bf, :, :],
                                    in_=w1stg[:, bf, :, :])
                    a_1c.inc(v)

            for oc in range(8):
                scalar.wait_ge(ld_2s.h, 16 * (oc + 1))
                if oc >= 1:
                    scalar.wait_ge(s_2b.h, 16 * oc)
                v = scalar.copy(out=w2c[:, 0, :, :], in_=w2stg[:, :, :])
                a_2c.inc(v)
            for j in range(NCH):
                bf = j % 2
                if j >= 2:
                    scalar.wait_ge(d_o.h, 8 * (j - 1))
                    scalar.wait_ge(p_s2.h, 2 * (j - 1))
                for c in range(8):
                    scalar.wait_ge(d_r2.h, 8 * j + c + 1)
                    v = scalar.activation(out=sqC[:, bf, c, :],
                                          in_=r2_sb[:, bf, c, :],
                                          func=AF.Square)
                a_q2.inc(v)
                scalar.wait_ge(d_w2.h, 3 * j + 1)
                scalar.activation(out=rows[:, 3, :], in_=rows[:, 2, :],
                                  func=AF.Ln)
                v = scalar.activation(out=rows[:, 2, :], in_=rows[:, 3, :],
                                      func=AF.Exp, scale=-0.5)
                d_w2.inc(v)
                scalar.wait_ge(p_p2.h, 2 * (j + 1))
                v = scalar.copy(out=rep1[:], in_=ps_r1[:])
                a_p2.inc(v)
                v = scalar.copy(out=rep2[:], in_=ps_r2[:])
                a_p2.inc(v)

        @block.vector
        def _(vector):
            vector.memset(ones_sb[:], 1.0)
            vector.memset(onesc_sb[:], 1.0)
            for j in range(NCH):
                bf = j % 2
                vector.wait_ge(ld_c.h, 32 * (j + 1))
                if j >= 2:
                    vector.wait_ge(p_st.h, 2 * (j - 1))
                    vector.wait_ge(d_x1.h, 8 * (j - 1))
                for oc in range(8):
                    vector.wait_ge(p_mo.h, 8 * j + oc + 1)
                    v = vector.scalar_tensor_tensor(
                        out=mo_sb[:, bf, oc, :], in0=x_sb[:, bf, oc, :],
                        scalar=2.0, in1=ps_mm[oc % 2][:],
                        op0=ALU.mult, op1=ALU.add)
                    d_mo.inc(v)
                vector.wait_ge(p_st.h, 2 * (j + 1))
                vector.tensor_scalar(out=rows[:, 0, :], in0=ps_s1[:],
                                     scalar1=1.0 / DM, scalar2=None,
                                     op0=ALU.mult)
                vector.tensor_tensor(out=rows[:, 1, :], in0=rows[:, 0, :],
                                     in1=rows[:, 0, :], op=ALU.mult)
                vector.tensor_scalar_add(out=rows[:, 1, :],
                                         in0=rows[:, 1, :],
                                         scalar1=-EPS_LN)
                v = vector.scalar_tensor_tensor(out=rows[:, 2, :],
                                                in0=ps_s2[:], scalar=1.0 / DM,
                                                in1=rows[:, 1, :],
                                                op0=ALU.mult, op1=ALU.subtract)
                d_rw.inc(v)
                vector.wait_ge(d_rw.h, 3 * j + 2)
                v = vector.tensor_tensor(out=rows[:, 1, :], in0=rows[:, 0, :],
                                         in1=rows[:, 2, :], op=ALU.mult)
                d_rw.inc(v)
                vector.wait_ge(a_rp.h, 2 * (j + 1))
                if j > 0:
                    vector.wait_ge(s_x1.h, 16 * j)
                for oc in range(8):
                    vector.tensor_tensor(out=sqA[:, bf, oc, :],
                                         in0=mo_sb[:, bf, oc, :],
                                         in1=rep1[:], op=ALU.mult)
                    v = vector.tensor_tensor(out=x1cA[:, oc, :],
                                             in0=sqA[:, bf, oc, :],
                                             in1=rep2[:], op=ALU.subtract)
                    d_x1.inc(v)
            for th in range(2):
                for ok in range(32):
                    i = 32 * th + ok
                    bf = i % 2
                    if i >= 2:
                        vector.wait_ge(s_h1.h, 16 * (i - 1))
                    for tch in range(4):
                        vector.wait_ge(p_h1.h, 4 * i + tch + 1)
                        v = vector.tensor_scalar(
                            out=h1o[:, bf, tch, :], in0=ps_mm[tch % 2][:],
                            scalar1=b1_sb[:, ok:ok + 1], scalar2=None,
                            op0=ALU.add)
                        a_h1.inc(v)
            for j in range(NCH):
                bf = j % 2
                vector.wait_ge(ld_2.h, 160 * j + 32)
                if j >= 2:
                    vector.wait_ge(p_s2.h, 2 * (j - 1))
                    vector.wait_ge(d_o.h, 8 * (j - 1))
                for oc in range(8):
                    vector.wait_ge(p_h2.h, 8 * j + oc + 1)
                    v = vector.tensor_scalar(
                        out=r2_sb[:, bf, oc, :], in0=ps_mm[oc % 2][:],
                        scalar1=b2_sb[:, oc:oc + 1], scalar2=None,
                        op0=ALU.add)
                    a_h2.inc(v)
                    v = vector.tensor_add(out=r2_sb[:, bf, oc, :],
                                          in0=r2_sb[:, bf, oc, :],
                                          in1=x1cC[:, oc, :])
                    d_r2.inc(v)
                vector.wait_ge(p_s2.h, 2 * (j + 1))
                vector.tensor_scalar(out=rows[:, 0, :], in0=ps_s1[:],
                                     scalar1=1.0 / DM, scalar2=None,
                                     op0=ALU.mult)
                vector.tensor_tensor(out=rows[:, 1, :], in0=rows[:, 0, :],
                                     in1=rows[:, 0, :], op=ALU.mult)
                vector.tensor_scalar_add(out=rows[:, 1, :],
                                         in0=rows[:, 1, :],
                                         scalar1=-EPS_LN)
                v = vector.scalar_tensor_tensor(out=rows[:, 2, :],
                                                in0=ps_s2[:], scalar=1.0 / DM,
                                                in1=rows[:, 1, :],
                                                op0=ALU.mult, op1=ALU.subtract)
                d_w2.inc(v)
                vector.wait_ge(d_w2.h, 3 * j + 2)
                v = vector.tensor_tensor(out=rows[:, 1, :], in0=rows[:, 0, :],
                                         in1=rows[:, 2, :], op=ALU.mult)
                d_w2.inc(v)
                vector.wait_ge(a_p2.h, 2 * (j + 1))
                if j > 0:
                    vector.wait_ge(s_o.h, 16 * j)
                for oc in range(8):
                    vector.tensor_tensor(out=sqC[:, bf, oc, :],
                                         in0=r2_sb[:, bf, oc, :],
                                         in1=rep1[:], op=ALU.mult)
                    v = vector.tensor_tensor(out=o_sb[:, oc, :],
                                             in0=sqC[:, bf, oc, :],
                                             in1=rep2[:], op=ALU.subtract)
                    d_o.inc(v)

        @block.tensor
        def _(tensor):
            def stats_a(j):
                bfj = j % 2
                tensor.wait_ge(a_sq.h, j + 1)
                if j > 0:
                    tensor.wait_ge(d_rw.h, 3 * (j - 1) + 1)
                for c in range(8):
                    v = tensor.matmul(ps_s1[:], onesc_sb[:],
                                      mo_sb[:, bfj, c, :],
                                      start=(c == 0), stop=(c == 7))
                p_st.inc(v)
                for c in range(8):
                    v = tensor.matmul(ps_s2[:], onesc_sb[:],
                                      sqA[:, bfj, c, :],
                                      start=(c == 0), stop=(c == 7))
                p_st.inc(v)
                tensor.wait_ge(d_rw.h, 3 * (j + 1))
                if j > 0:
                    tensor.wait_ge(a_rp.h, 2 * j)
                v = tensor.matmul(ps_r1[:], ones_sb[:], rows[:, 2, :],
                                  start=True, stop=True)
                p_rp.inc(v)
                v = tensor.matmul(ps_r2[:], ones_sb[:], rows[:, 1, :],
                                  start=True, stop=True)
                p_rp.inc(v)

            for j in range(NCH):
                bf = j % 2
                tensor.wait_ge(ld_c.h, 32 * (j + 1))
                tensor.wait_ge(a_wc.h, 4)
                for oc in range(8):
                    if 8 * j + oc >= 2:
                        tensor.wait_ge(d_mo.h, 8 * j + oc - 1)
                    for k in range(16):
                        v = tensor.matmul(ps_mm[oc % 2][:],
                                          wo_sb[:, k, oc * 128:(oc + 1) * 128],
                                          g_sb[:, bf, k, :],
                                          start=(k == 0), stop=(k == 15))
                    p_mo.inc(v)
                    if oc == 1 and j > 0:
                        stats_a(j - 1)
            stats_a(NCH - 1)
            tensor.wait_ge(d_x1.h, A_X1)
            for th in range(2):
                tensor.wait_ge(a_xh.h, th + 1)
                for ok in range(32):
                    i = 32 * th + ok
                    bf = i % 2
                    tensor.wait_ge(a_1c.h, i + 1)
                    for tch in range(4):
                        if 4 * i + tch >= 2:
                            tensor.wait_ge(a_h1.h, 4 * i + tch - 1)
                        for k in range(8):
                            v = tensor.matmul(
                                ps_mm[tch % 2][:], w1c[:, bf, k, :],
                                x1bh[:, k, tch * TC:(tch + 1) * TC],
                                start=(k == 0), stop=(k == 7))
                        p_h1.inc(v)
            def stats_c(j):
                bfj = j % 2
                tensor.wait_ge(a_q2.h, j + 1)
                if j > 0:
                    tensor.wait_ge(d_w2.h, 3 * (j - 1) + 1)
                for c in range(8):
                    v = tensor.matmul(ps_s1[:], onesc_sb[:],
                                      r2_sb[:, bfj, c, :],
                                      start=(c == 0), stop=(c == 7))
                p_s2.inc(v)
                for c in range(8):
                    v = tensor.matmul(ps_s2[:], onesc_sb[:],
                                      sqC[:, bfj, c, :],
                                      start=(c == 0), stop=(c == 7))
                p_s2.inc(v)
                tensor.wait_ge(d_w2.h, 3 * (j + 1))
                if j > 0:
                    tensor.wait_ge(a_p2.h, 2 * j)
                v = tensor.matmul(ps_r1[:], ones_sb[:], rows[:, 2, :],
                                  start=True, stop=True)
                p_p2.inc(v)
                v = tensor.matmul(ps_r2[:], ones_sb[:], rows[:, 1, :],
                                  start=True, stop=True)
                p_p2.inc(v)

            for j in range(NCH):
                for oc in range(8):
                    tensor.wait_ge(ld_2.h, 160 * j + 32 + 16 * (oc + 1))
                    if 8 * j + oc >= 2:
                        tensor.wait_ge(a_h2.h, 8 * j + oc - 1)
                    for k in range(32):
                        v = tensor.matmul(ps_mm[oc % 2][:],
                                          w2c[:, (8 * j + oc) % 2, k, :],
                                          h1t[:, k, :],
                                          start=(k == 0), stop=(k == 31))
                    p_h2.inc(v)
                    if oc == 1 and j > 0:
                        stats_c(j - 1)
            stats_c(NCH - 1)

        @block.gpsimd
        def _(g):
            for j in range(NCH):
                g.wait_ge(d_x1.h, 8 * (j + 1))
                v = g.dma_start(
                    out=x1f_d[:, :].rearrange("(c p) t -> p c t", p=128)[
                        :, :, j * TC:(j + 1) * TC],
                    in_=x1cA[:, :, :])
                s_x1.inc(v, 16)
            for th in range(2):
                for ok in range(32):
                    i = 32 * th + ok
                    bf = i % 2
                    g.wait_ge(a_h1.h, 4 * (i + 1))
                    v = g.dma_start(
                        out=h1_d[ok * 128:(ok + 1) * 128,
                                 th * TH:(th + 1) * TH],
                        in_=h1o[:, bf, :, :].rearrange("p a b -> p (a b)"))
                    s_h1.inc(v, 16)
            for oc in range(8):
                g.wait_ge(a_2c.h, oc + 1)
                v = g.dma_start(
                    out=w2b_d[:, :].rearrange("(c p) o -> p c o", p=128)[
                        :, :, oc * 128:(oc + 1) * 128],
                    in_=w2c[:, 0, :, :])
                s_2b.inc(v, 16)
            for j in range(NCH):
                g.wait_ge(d_o.h, 8 * (j + 1))
                v = g.dma_start(
                    out=out_d[:, :].rearrange("(c p) t -> p c t", p=128)[
                        :, :, j * TC:(j + 1) * TC],
                    in_=o_sb[:, :, :])
                s_o.inc(v, 16)
            g.wait_ge(s_o.h, s_o.n)

    return nc


def _t(a):
    return np.ascontiguousarray(np.asarray(a).T)


def kernel(L0_attn_x, L0_attn_g, L0_attn_W_in, L0_attn_W_out, L0_attn_conv_w,
           L0_attn_conv_b, L0_attn_W_x, L0_attn_W_dt, L0_attn_b_dt,
           L0_attn_A_log, L0_attn_D, L0_ffn_W1, L0_ffn_b1, L0_ffn_W2,
           L0_ffn_b2):
    x = np.asarray(L0_attn_x, np.float32)
    x_fm = [np.ascontiguousarray(x[b].T) for b in range(B)]

    winT = _t(L0_attn_W_in)
    wxT = _t(L0_attn_W_x)
    gvec = np.ascontiguousarray(
        np.asarray(L0_attn_g, np.float32).reshape(8, 128).T)
    conv_w = np.asarray(L0_attn_conv_w, np.float32)
    cdiag = np.zeros((128, 64, 128), np.float32)
    for dc in range(16):
        for k in range(4):
            np.fill_diagonal(cdiag[:, 4 * dc + k, :],
                             conv_w[dc * 128:(dc + 1) * 128, k])
    cdiag = np.ascontiguousarray(cdiag.reshape(128, 64 * 128))
    cb = np.ascontiguousarray(
        np.asarray(L0_attn_conv_b, np.float32).reshape(16, 128).T)
    ident = np.eye(128, dtype=np.float32)

    nc1 = build_l1()
    in_maps1 = []
    zero3 = np.zeros((DM, 3), np.float32)
    for c in range(N_CORES):
        b, q = divmod(c, 4)
        xh = (zero3 if q == 0 else
              np.ascontiguousarray(x_fm[b][:, q * TQ - 3:q * TQ]))
        in_maps1.append({
            "xq": np.ascontiguousarray(x_fm[b][:, q * TQ:(q + 1) * TQ]),
            "xh": xh,
            "winT": winT, "wxT": wxT, "gvec": gvec, "cdiag": cdiag,
            "cb": cb, "ident": ident,
        })
    r1 = run_bass_kernel_spmd(nc1, in_maps1, core_ids=list(range(N_CORES)))
    xc_all = [np.concatenate([r1.results[4 * b + q]["xc"] for q in range(4)],
                             axis=1) for b in range(B)]
    z_all = [np.concatenate([r1.results[4 * b + q]["z"] for q in range(4)],
                            axis=1) for b in range(B)]
    par_all = [np.concatenate([r1.results[4 * b + q]["par"] for q in range(4)],
                              axis=1) for b in range(B)]

    A_log = np.asarray(L0_attn_A_log, np.float32)
    W_dt = np.asarray(L0_attn_W_dt, np.float32)
    b_dt = np.asarray(L0_attn_b_dt, np.float32)
    Dv = np.asarray(L0_attn_D, np.float32)
    sel = np.zeros((32, 32, 128), np.float32)
    for r in range(32):
        sel[r, r, :] = 1.0
    sel = np.ascontiguousarray(sel.reshape(32, 32 * 128))
    nc2 = build_l2()
    in_maps2 = []
    for c in range(N_CORES):
        b, jg = divmod(c, 4)
        sl = slice(jg * CG, (jg + 1) * CG)
        in_maps2.append({
            "xc": np.ascontiguousarray(xc_all[b][sl]),
            "z": np.ascontiguousarray(z_all[b][sl]),
            "par": par_all[b],
            "alog": np.ascontiguousarray(
                A_log[sl].reshape(4, 128, DS).transpose(1, 0, 2)
                .reshape(128, 4 * DS)),
            "wdt": np.ascontiguousarray(W_dt[sl, 0].reshape(1, CG)),
            "bdt": np.ascontiguousarray(b_dt[sl].reshape(4, 128).T),
            "dvec": np.ascontiguousarray(Dv[sl].reshape(4, 128).T),
            "sel": sel,
        })
    r2 = run_bass_kernel_spmd(nc2, in_maps2, core_ids=list(range(N_CORES)))
    g_all = [np.concatenate([r2.results[4 * b + jg]["gact"] for jg in range(4)],
                            axis=0) for b in range(B)]

    woT = _t(L0_attn_W_out)
    W1 = np.asarray(L0_ffn_W1, np.float32)
    b1 = np.asarray(L0_ffn_b1, np.float32)
    W2 = np.asarray(L0_ffn_W2, np.float32)
    b2 = np.asarray(L0_ffn_b2, np.float32)
    nc3 = build_l3()
    in_maps3 = []
    for c in range(N_CORES):
        b, q = divmod(c, 4)
        grp = 4 * b + q
        in_maps3.append({
            "gall": np.ascontiguousarray(g_all[b][:, q * TQ:(q + 1) * TQ]),
            "xq": np.ascontiguousarray(x_fm[b][:, q * TQ:(q + 1) * TQ]),
            "woT": woT,
            "w1T": np.ascontiguousarray(W1[grp].T),
            "b1": np.ascontiguousarray(b1[grp].reshape(32, 128).T),
            "w2T": np.ascontiguousarray(W2[grp].T),
            "b2": np.ascontiguousarray(b2[grp].reshape(8, 128).T),
        })
    r3 = run_bass_kernel_spmd(nc3, in_maps3, core_ids=list(range(N_CORES)))
    out = np.empty((B, T, DM), np.float32)
    for c in range(N_CORES):
        b, q = divmod(c, 4)
        out[b, q * TQ:(q + 1) * TQ, :] = r3.results[c]["out"].T
    return out


# version 12
# speedup vs baseline: 1.0334x; 1.0334x over previous
import math
from contextlib import ExitStack

import numpy as np
import ml_dtypes

import concourse.bass as bass
import concourse.mybir as mybir
from concourse.bass_utils import run_bass_kernel_spmd

F32 = mybir.dt.float32
BF16 = mybir.dt.bfloat16
AF = mybir.ActivationFunctionType
ALU = mybir.AluOpType
BFNP = ml_dtypes.bfloat16

B, T, DM = 2, 16384, 1024
DI, DS, DCONV = 2048, 16, 4
TQ = T // 4
CG = DI // 4
EPS_RMS = 1e-6
EPS_LN = 1e-5
N_CORES = 8


class Sem:

    def __init__(self, es, nc, name):
        self.h = es.enter_context(nc.semaphore(name))
        self.n = 0

    def inc(self, instr, k=1):
        instr.then_inc(self.h, k)
        self.n += k
        return self.n


def _sb(es, nc, name, shape, dtype):
    return es.enter_context(nc.sbuf_tensor(name, shape, dtype))


def _ps(es, nc, name, shape, dtype=F32):
    return es.enter_context(nc.psum_tensor(name, shape, dtype))


class Arena:

    def __init__(self, es, nc, name, words):
        self.t = es.enter_context(nc.sbuf_tensor(name, [128, words], F32))

    def view(self, off, shape, dtype):
        elems = 1
        for s in shape[1:]:
            elems *= s
        words = elems * (2 if dtype == BF16 else 4) // 4
        ap = self.t[:, off:off + words]
        if dtype == BF16:
            ap = ap.bitcast(BF16)
        if len(shape) == 2:
            return ap
        if len(shape) == 3:
            return ap.rearrange("p (a b) -> p a b", a=shape[1])
        if len(shape) == 4:
            return ap.rearrange("p (a b c) -> p a b c", a=shape[1], b=shape[2])
        raise ValueError(shape)


def build_l1():
    nc = bass.Bass("TRN2", target_bir_lowering=False)
    TC = 512
    NCH = TQ // TC

    xq_d = nc.dram_tensor("xq", [DM, TQ], F32, kind="ExternalInput")
    xh_d = nc.dram_tensor("xh", [DM, 3], F32, kind="ExternalInput")
    winT_d = nc.dram_tensor("winT", [DM, 2 * DI], F32, kind="ExternalInput")
    wxT_d = nc.dram_tensor("wxT", [DI, 33], F32, kind="ExternalInput")
    g_d = nc.dram_tensor("gvec", [128, 8], F32, kind="ExternalInput")
    cdiag_d = nc.dram_tensor("cdiag", [128, 64 * 128], F32, kind="ExternalInput")
    cb_d = nc.dram_tensor("cb", [128, 16], F32, kind="ExternalInput")
    ident_d = nc.dram_tensor("ident", [128, 128], F32, kind="ExternalInput")

    xc_d = nc.dram_tensor("xc", [DI, TQ], BF16, kind="ExternalOutput")
    z_d = nc.dram_tensor("z", [DI, TQ], BF16, kind="ExternalOutput")
    par_d = nc.dram_tensor("par", [33, TQ], F32, kind="ExternalOutput")

    es = ExitStack()
    with es:
        w_sb = _sb(es, nc, "w_sb", [128, 8, 2 * DI], BF16)
        wstg = _sb(es, nc, "wstg", [128, 8, 512], F32)
        wx_sb = _sb(es, nc, "wx_sb", [128, 16, 33], BF16)
        wxstg = _sb(es, nc, "wxstg", [128, 16, 33], F32)
        cd_sb = _sb(es, nc, "cd_sb", [128, 64, 128], BF16)
        cb_sb = _sb(es, nc, "cb_sb", [128, 16], F32)
        g_sb = _sb(es, nc, "g_sb", [128, 8], F32)
        id_sb = _sb(es, nc, "id_sb", [128, 128], F32)
        ones_sb = _sb(es, nc, "ones_sb", [1, 128], F32)
        onesc_sb = _sb(es, nc, "onesc_sb", [128, 1], F32)
        sx = _sb(es, nc, "sx", [128, 8, TC], F32)
        sxg = _sb(es, nc, "sxg", [128, 8, TC], BF16)
        ssq = _sb(es, nc, "ssq", [128, 8, TC], F32)
        rowa = _sb(es, nc, "rowa", [1, TC], F32)
        rowb = _sb(es, nc, "rowb", [1, TC], F32)
        rstd_rep = _sb(es, nc, "rstd_rep", [128, TC], F32)
        xs_buf = _sb(es, nc, "xs_buf", [128, 16, TC + 3], BF16)
        z_sb = _sb(es, nc, "z_sb", [128, 16, TC], BF16)
        xc_sb = _sb(es, nc, "xc_sb", [128, 16, TC], BF16)
        par_tm = _sb(es, nc, "par_tm", [128, 2, 33], F32)
        sxh = _sb(es, nc, "sxh", [128, 8, 3], F32)
        sxgh = _sb(es, nc, "sxgh", [128, 8, 3], BF16)
        sqh = _sb(es, nc, "sqh", [128, 8, 3], F32)
        rsh = _sb(es, nc, "rsh", [128, 3], F32)
        rowh = _sb(es, nc, "rowh", [1, 3], F32)
        par_fm = _sb(es, nc, "par_fm", [33, TC], F32)

        ps_ss = _ps(es, nc, "ps_ss", [1, TC])
        ps_rep = _ps(es, nc, "ps_rep", [128, TC])
        ps_xz = [_ps(es, nc, f"ps_xz{i}", [128, TC]) for i in range(2)]
        ps_cv = [_ps(es, nc, f"ps_cv{i}", [128, TC]) for i in range(2)]
        ps_par = _ps(es, nc, "ps_par", [128, 33])
        ps_tr = _ps(es, nc, "ps_tr", [33, 128])

        ld_w = Sem(es, nc, "ld_w")
        ld_x = Sem(es, nc, "ld_x")
        a_wc = Sem(es, nc, "a_wc")
        a_xg = Sem(es, nc, "a_xg")
        p_ss = Sem(es, nc, "p_ss")
        d_row = Sem(es, nc, "d_row")
        a_row = Sem(es, nc, "a_row")
        p_rep = Sem(es, nc, "p_rep")
        a_rep = Sem(es, nc, "a_rep")
        p_xz = Sem(es, nc, "p_xz")
        d_xz = Sem(es, nc, "d_xz")
        p_cv = Sem(es, nc, "p_cv")
        a_cv = Sem(es, nc, "a_cv")
        p_par = Sem(es, nc, "p_par")
        a_par = Sem(es, nc, "a_par")
        p_tr = Sem(es, nc, "p_tr")
        a_tr = Sem(es, nc, "a_tr")
        g_out = Sem(es, nc, "g_out")
        halo = Sem(es, nc, "halo")
        ld_h = Sem(es, nc, "ld_h")

        block = es.enter_context(nc.Block())

        @block.sync
        def _(sync):
            for sl in range(8):
                sync.dma_start(
                    out=wstg[:, :, :],
                    in_=winT_d[:, :].rearrange("(c p) o -> p c o", p=128)[
                        :, :, sl * 512:(sl + 1) * 512],
                ).then_inc(ld_w.h, 16)
                ld_w.n += 16
                sync.wait_ge(a_wc.h, sl + 1)
            for h in range(2):
                sync.dma_start(
                    out=wstg[:, :, :],
                    in_=cdiag_d[:, h * 4096:(h + 1) * 4096].rearrange(
                        "p (n o) -> p n o", o=512),
                ).then_inc(ld_w.h, 16)
                ld_w.n += 16
                sync.wait_ge(a_wc.h, 9 + h)
            for src, dst in ((wxT_d[:, :].rearrange("(c p) o -> p c o", p=128),
                              wxstg[:, :, :]),
                             (cb_d[:, :], cb_sb[:]),
                             (g_d[:, :], g_sb[:]),
                             (ident_d[:, :], id_sb[:])):
                sync.dma_start(out=dst, in_=src).then_inc(ld_w.h, 16)
                ld_w.n += 16
            sync.dma_start(
                out=sxh[:, :, :],
                in_=xh_d[:, :].rearrange("(c p) t -> p c t", p=128),
            ).then_inc(ld_h.h, 16)
            ld_h.n += 16
            for j in range(NCH):
                sync.dma_start(
                    out=sx[:, :, :],
                    in_=xq_d[:, :].rearrange("(c p) t -> p c t", p=128)[
                        :, :, j * TC:(j + 1) * TC],
                ).then_inc(ld_x.h, 16)
                ld_x.n += 16
                if j + 1 < NCH:
                    sync.wait_ge(a_xg.h, 2 * (j + 1))

        @block.vector
        def _(vector):
            vector.memset(ones_sb[:], 1.0)
            v = vector.memset(onesc_sb[:], 1.0)
            halo.inc(v)
            vector.wait_ge(halo.h, 3)
            v = vector.tensor_scalar(
                out=rowh[:], in0=ps_ss[0:1, 0:3], scalar1=1.0 / DM,
                scalar2=EPS_RMS, op0=ALU.mult, op1=ALU.add)
            halo.inc(v)
            for oc in range(16):
                vector.wait_ge(halo.h, 8 + 2 * oc)
                v = vector.tensor_tensor(out=xs_buf[:, oc, 0:3],
                                         in0=ps_xz[oc % 2][0:128, 0:3],
                                         in1=rsh[:], op=ALU.mult)
                halo.inc(v)
            for j in range(NCH):
                vector.wait_ge(p_ss.h, j + 1)
                if j > 0:
                    vector.wait_ge(p_rep.h, j)
                v = vector.tensor_scalar(
                    out=rowa[:], in0=ps_ss[:], scalar1=1.0 / DM,
                    scalar2=EPS_RMS, op0=ALU.mult, op1=ALU.add)
                d_row.inc(v)
                if j > 0:
                    vector.wait_ge(p_cv.h, 16 * j)
                    for dc in range(16):
                        v = vector.tensor_copy(out=xs_buf[:, dc, 0:3],
                                               in_=xs_buf[:, dc, TC:TC + 3])
                        d_xz.inc(v)
                else:
                    for dc in range(16):
                        v = vector.engine_nop()
                        d_xz.inc(v)
                vector.wait_ge(a_rep.h, j + 1)
                if j > 0:
                    vector.wait_ge(g_out.h, 48 * (j - 1) + 32)
                for oc in range(32):
                    vector.wait_ge(p_xz.h, 32 * j + oc + 1)
                    dst = (xs_buf[:, oc, 3:3 + TC] if oc < 16
                           else z_sb[:, oc - 16, :])
                    v = vector.tensor_tensor(out=dst, in0=ps_xz[oc % 2][:],
                                             in1=rstd_rep[:], op=ALU.mult)
                    d_xz.inc(v)

        @block.scalar
        def _(scalar):
            for sl in range(8):
                scalar.wait_ge(ld_w.h, 16 * (sl + 1))
                v = scalar.copy(out=w_sb[:, :, sl * 512:(sl + 1) * 512],
                                in_=wstg[:, :, :])
                a_wc.inc(v)
            for h in range(2):
                scalar.wait_ge(ld_w.h, 16 * (9 + h))
                v = scalar.copy(
                    out=cd_sb[:, h * 32:(h + 1) * 32, :].rearrange(
                        "p n o -> p (n o)").rearrange("p (n o) -> p n o", o=512),
                    in_=wstg[:, :, :])
                a_wc.inc(v)
            scalar.wait_ge(ld_w.h, 14 * 16)
            scalar.copy(out=wx_sb[:, :, :], in_=wxstg[:, :, :])
            scalar.wait_ge(ld_h.h, 16)
            scalar.wait_ge(ld_w.h, 13 * 16)
            for c in range(8):
                scalar.activation(out=sxgh[:, c, :], in_=sxh[:, c, :],
                                  func=AF.Copy, scale=g_sb[:, c:c + 1])
            for c in range(8):
                v = scalar.activation(out=sqh[:, c, :], in_=sxh[:, c, :],
                                      func=AF.Square)
            halo.inc(v)
            scalar.wait_ge(halo.h, 4)
            scalar.activation(out=rowb[0:1, 0:3], in_=rowh[:], func=AF.Ln)
            v = scalar.activation(out=rowh[:], in_=rowb[0:1, 0:3],
                                  func=AF.Exp, scale=-0.5)
            halo.inc(v)
            scalar.wait_ge(halo.h, 6)
            v = scalar.copy(out=rsh[:], in_=ps_rep[0:128, 0:3])
            halo.inc(v)
            for j in range(NCH):
                scalar.wait_ge(ld_x.h, 16 * (j + 1))
                if j > 0:
                    scalar.wait_ge(p_xz.h, 32 * j)
                for c in range(8):
                    v = scalar.activation(out=sxg[:, c, :], in_=sx[:, c, :],
                                          func=AF.Copy, scale=g_sb[:, c:c + 1])
                a_xg.inc(v)
                if j > 0:
                    scalar.wait_ge(p_ss.h, j)
                for c in range(8):
                    v = scalar.activation(out=ssq[:, c, :], in_=sx[:, c, :],
                                          func=AF.Square)
                a_xg.inc(v)
                scalar.wait_ge(d_row.h, j + 1)
                scalar.activation(out=rowb[:], in_=rowa[:], func=AF.Ln)
                v = scalar.activation(out=rowa[:], in_=rowb[:], func=AF.Exp,
                                      scale=-0.5)
                a_row.inc(v)
                scalar.wait_ge(p_rep.h, j + 1)
                if j > 0:
                    scalar.wait_ge(d_xz.h, 48 * j)
                v = scalar.copy(out=rstd_rep[:], in_=ps_rep[:])
                a_rep.inc(v)
                if j > 0:
                    scalar.wait_ge(p_par.h, 4 * j)
                    scalar.wait_ge(g_out.h, 48 * (j - 1) + 16)
                for dc in range(16):
                    scalar.wait_ge(p_cv.h, 16 * j + dc + 1)
                    v = scalar.activation(out=xc_sb[:, dc, :],
                                          in_=ps_cv[dc % 2][:], func=AF.Silu,
                                          bias=cb_sb[:, dc:dc + 1])
                    a_cv.inc(v)
                if j > 0:
                    scalar.wait_ge(g_out.h, 48 * j)
                for tt in range(4):
                    scalar.wait_ge(p_par.h, 4 * j + tt + 1)
                    v = scalar.copy(out=par_tm[:, tt % 2, :], in_=ps_par[:])
                    a_par.inc(v)
                    scalar.wait_ge(p_tr.h, 4 * j + tt + 1)
                    v = scalar.copy(out=par_fm[:, tt * 128:(tt + 1) * 128],
                                    in_=ps_tr[:])
                    a_tr.inc(v)

        @block.tensor
        def _(tensor):
            tensor.wait_ge(halo.h, 2)
            for c in range(8):
                v = tensor.matmul(ps_ss[0:1, 0:3], onesc_sb[:], sqh[:, c, :],
                                  start=(c == 0), stop=(c == 7))
            halo.inc(v)
            tensor.wait_ge(halo.h, 5)
            v = tensor.matmul(ps_rep[0:128, 0:3], ones_sb[:], rowh[:],
                              start=True, stop=True)
            halo.inc(v)
            tensor.wait_ge(halo.h, 7)
            for oc in range(16):
                if oc >= 2:
                    tensor.wait_ge(halo.h, 9 + 2 * (oc - 2))
                for c in range(8):
                    v = tensor.matmul(ps_xz[oc % 2][0:128, 0:3],
                                      w_sb[:, c, oc * 128:(oc + 1) * 128],
                                      sxgh[:, c, :],
                                      start=(c == 0), stop=(c == 7))
                halo.inc(v)
            tensor.wait_ge(halo.h, 39)
            for j in range(NCH):
                tensor.wait_ge(a_xg.h, 2 * (j + 1))
                if j > 0:
                    tensor.wait_ge(d_row.h, j)
                for c in range(8):
                    v = tensor.matmul(ps_ss[:], onesc_sb[:], ssq[:, c, :],
                                      start=(c == 0), stop=(c == 7))
                p_ss.inc(v)
                tensor.wait_ge(a_row.h, j + 1)
                if j > 0:
                    tensor.wait_ge(a_rep.h, j)
                v = tensor.matmul(ps_rep[:], ones_sb[:], rowa[:],
                                  start=True, stop=True)
                p_rep.inc(v)
                for oc in range(32):
                    if j > 0 or oc >= 2:
                        if oc >= 2:
                            need = 48 * j + 16 + (oc - 2) + 1
                        else:
                            need = 48 * (j - 1) + 16 + (30 + oc) + 1
                        tensor.wait_ge(d_xz.h, need)
                    for c in range(8):
                        v = tensor.matmul(
                            ps_xz[oc % 2][:],
                            w_sb[:, c, oc * 128:(oc + 1) * 128],
                            sxg[:, c, :], start=(c == 0), stop=(c == 7))
                    p_xz.inc(v)
                for dc in range(16):
                    tensor.wait_ge(d_xz.h, 48 * j + 16 + dc + 1)
                    if j > 0 or dc >= 2:
                        tensor.wait_ge(a_cv.h, 16 * j + dc - 1)
                    for k in range(4):
                        v = tensor.matmul(ps_cv[dc % 2][:],
                                          cd_sb[:, 4 * dc + k, :],
                                          xs_buf[:, dc, k:k + TC],
                                          start=(k == 0), stop=(k == 3))
                    p_cv.inc(v)
                tensor.wait_ge(a_cv.h, 16 * (j + 1))
                for tt in range(4):
                    if j > 0 or tt > 0:
                        tensor.wait_ge(a_par.h, 4 * j + tt)
                    for dc in range(16):
                        v = tensor.matmul(
                            ps_par[:], xc_sb[:, dc, tt * 128:(tt + 1) * 128],
                            wx_sb[:, dc, :], start=(dc == 0), stop=(dc == 15))
                    p_par.inc(v)
                    tensor.wait_ge(a_par.h, 4 * j + tt + 1)
                    if j > 0 or tt > 0:
                        tensor.wait_ge(a_tr.h, 4 * j + tt)
                    v = tensor.transpose(ps_tr[:], par_tm[:, tt % 2, :],
                                         id_sb[:])
                    p_tr.inc(v)

        @block.gpsimd
        def _(g):
            for j in range(NCH):
                g.wait_ge(a_cv.h, 16 * (j + 1))
                v = g.dma_start(
                    out=xc_d[:, :].rearrange("(c p) t -> p c t", p=128)[
                        :, :, j * TC:(j + 1) * TC],
                    in_=xc_sb[:, :, :])
                g_out.inc(v, 16)
                g.wait_ge(d_xz.h, 48 * (j + 1))
                v = g.dma_start(
                    out=z_d[:, :].rearrange("(c p) t -> p c t", p=128)[
                        :, :, j * TC:(j + 1) * TC],
                    in_=z_sb[:, :, :])
                g_out.inc(v, 16)
                g.wait_ge(a_tr.h, 4 * (j + 1))
                v = g.dma_start(out=par_d[:, j * TC:(j + 1) * TC],
                                in_=par_fm[:, :])
                g_out.inc(v, 16)
            g.wait_ge(g_out.h, g_out.n)

    return nc


def build_l2():
    nc = bass.Bass("TRN2", target_bir_lowering=False)
    TC = 1024
    TB = TC + 2
    BL = 4 * TB
    FD4 = 4 * TB - 2
    NCH = T // TC

    xc_d = nc.dram_tensor("xc", [CG, T], BF16, kind="ExternalInput")
    z_d = nc.dram_tensor("z", [CG, T], BF16, kind="ExternalInput")
    par_d = nc.dram_tensor("par", [33, T], F32, kind="ExternalInput")
    alog_d = nc.dram_tensor("alog", [128, 4 * DS], F32, kind="ExternalInput")
    wdt_d = nc.dram_tensor("wdt", [1, CG], F32, kind="ExternalInput")
    bdt_d = nc.dram_tensor("bdt", [128, 4], F32, kind="ExternalInput")
    dvec_d = nc.dram_tensor("dvec", [128, 4], F32, kind="ExternalInput")
    sel_d = nc.dram_tensor("sel", [32, 32 * 128], F32, kind="ExternalInput")

    g_d = nc.dram_tensor("gact", [CG, T], BF16, kind="ExternalOutput")

    es = ExitStack()
    with es:
        xc_sb = _sb(es, nc, "xc_sb", [128, 2, 4, TC], BF16)
        z_sb = _sb(es, nc, "z_sb", [128, 4, TC], BF16)
        bc_sb = _sb(es, nc, "bc_sb", [33, 2, TC], F32)
        bcbf_sb = _sb(es, nc, "bcbf_sb", [32, 2, TC], BF16)
        dt_sb = _sb(es, nc, "dt_sb", [128, 4, TC], BF16)
        u_sb = _sb(es, nc, "u_sb", [128, 4, TC], BF16)
        sp_r = _sb(es, nc, "sp_r", [128, TC], F32)
        t_a = _sb(es, nc, "t_a", [128, TC], F32)
        brep = _sb(es, nc, "brep", [128, 2, 4, TC], BF16)
        crep = _sb(es, nc, "crep", [128, 2, 4, TC], BF16)
        a4 = _sb(es, nc, "a4", [128, 2, BL], F32)
        bu4 = _sb(es, nc, "bu4", [128, 2, BL], BF16)
        h4 = _sb(es, nc, "h4", [128, 2, BL], BF16)
        tmp4 = _sb(es, nc, "tmp4", [128, 4, TC], BF16)
        pair2 = _sb(es, nc, "pair2", [128, 2, TC], BF16)
        yac_sb = _sb(es, nc, "yac_sb", [128, 4, TC], BF16)
        t1_sb = _sb(es, nc, "t1_sb", [128, TC], F32)
        sz_sb = _sb(es, nc, "sz_sb", [128, TC], BF16)
        gout_sb = _sb(es, nc, "gout_sb", [128, 1, 4, TC], BF16)
        carry = _sb(es, nc, "carry", [128, 4, DS], F32)
        negA = _sb(es, nc, "negA", [128, 4, DS], F32)
        ea_sb = _sb(es, nc, "ea_sb", [128, 4 * DS], F32)
        wdt_sb = _sb(es, nc, "wdt_sb", [33, CG], F32)
        bdt_sb = _sb(es, nc, "bdt_sb", [128, 4], F32)
        dv_sb = _sb(es, nc, "dv_sb", [128, 4], F32)
        ones_sb = _sb(es, nc, "ones_sb", [1, 128], F32)
        sel_sb = _sb(es, nc, "sel_sb", [32, 32, 128], BF16)

        ps_dt = _ps(es, nc, "ps_dt", [128, TC])
        ps_bc = [_ps(es, nc, f"ps_bc{i}", [128, TC]) for i in range(2)]

        ld_w = Sem(es, nc, "ld_w")
        ld_c = Sem(es, nc, "ld_c")
        d_pre = Sem(es, nc, "d_pre")
        p_dt = Sem(es, nc, "p_dt")
        a_spp = Sem(es, nc, "a_spp")
        a_spl = Sem(es, nc, "a_spl")
        a_bcc = Sem(es, nc, "a_bcc")
        d_dt = Sem(es, nc, "d_dt")
        p_bc = Sem(es, nc, "p_bc")
        a_bc = Sem(es, nc, "a_bc")
        a_ex = Sem(es, nc, "a_ex")
        g_bu = Sem(es, nc, "g_bu")
        d_sc = Sem(es, nc, "d_sc")
        d_cy = Sem(es, nc, "d_cy")
        d_cm = Sem(es, nc, "d_cm")
        d_ga = Sem(es, nc, "d_ga")
        a_sz = Sem(es, nc, "a_sz")
        g_gt = Sem(es, nc, "g_gt")
        g_dma = Sem(es, nc, "g_dma")

        selstg = a4[0:32, 0, 0:4096].rearrange("p (n o) -> p n o", o=128)

        block = es.enter_context(nc.Block())
        LDW_TOTAL = 5 * 16

        def sent_pair(buf_ap):
            return buf_ap[:, TC:TC + 3 * TB].rearrange(
                "p (b w) -> p b w", w=TB)[:, :, 0:2]

        def sent_col(buf_ap, which):
            return buf_ap[:, TC:TC + 3 * TB].rearrange(
                "p (b w) -> p b w", w=TB)[:, :, which:which + 1]

        def tiles_view(buf_ap):
            return buf_ap.rearrange("p (b w) -> p b w", w=TB)[:, :, 0:TC]

        def carry_cols(hb):
            return bass.AP(h4, hb * BL + (TC - 1), [[2 * BL, 128], [TB, 4]])

        @block.sync
        def _(sync):
            for src, dst in ((alog_d[:, :], ea_sb[:]),
                             (wdt_d[:, :], wdt_sb[32:33, :]),
                             (bdt_d[:, :], bdt_sb[:]),
                             (sel_d[:, :].rearrange("p (n o) -> p n o", o=128),
                              selstg),
                             (dvec_d[:, :], dv_sb[:])):
                sync.dma_start(out=dst, in_=src).then_inc(ld_w.h, 16)
                ld_w.n += 16
            for j in range(NCH):
                bf = j % 2
                if j >= 2:
                    sync.wait_ge(g_dma.h, 16 * (j - 1))
                sync.dma_start(
                    out=xc_sb[:, bf, :, :],
                    in_=xc_d[:, :].rearrange("(c p) t -> p c t", p=128)[
                        :, :, j * TC:(j + 1) * TC]).then_inc(ld_c.h, 16)
                sync.dma_start(out=bc_sb[:, bf, :],
                               in_=par_d[:, j * TC:(j + 1) * TC]
                               ).then_inc(ld_c.h, 16)
                if j >= 1:
                    sync.wait_ge(g_gt.h, 4 * j)
                sync.dma_start(
                    out=z_sb[:, :, :],
                    in_=z_d[:, :].rearrange("(c p) t -> p c t", p=128)[
                        :, :, j * TC:(j + 1) * TC]).then_inc(ld_c.h, 16)
                ld_c.n += 48

        @block.vector
        def _(vector):
            vector.memset(ones_sb[:], 1.0)
            vector.memset(carry[:], 0.0)
            vector.wait_ge(d_pre.h, 1)
            v = vector.tensor_scalar_mul(
                out=negA[:].rearrange("p a b -> p (a b)"), in0=ea_sb[:],
                scalar1=-1.0)
            d_pre.inc(v)
            for hb in range(2):
                vector.memset(sent_pair(a4[:, hb, :]), 0.0)
                vector.memset(sent_col(bu4[:, hb, :], 0), 0.0)
            for j in range(NCH):
                bf = j % 2
                if j > 0:
                    vector.wait_ge(a_ex.h, 16 * j)
                    vector.wait_ge(g_bu.h, 16 * j)
                for dc in range(4):
                    vector.wait_ge(a_spl.h, 4 * j + dc + 1)
                    v = vector.tensor_add(out=dt_sb[:, dc, :], in0=sp_r[:],
                                          in1=t_a[:])
                    vector.wait_ge(ld_c.h, 48 * j + 16)
                    v = vector.tensor_tensor(out=u_sb[:, dc, :],
                                             in0=dt_sb[:, dc, :],
                                             in1=xc_sb[:, bf, dc, :],
                                             op=ALU.mult)
                    d_dt.inc(v)
                def emit_bu(G2):
                    j2 = G2 // 16
                    sg2 = (G2 % 16) // 4
                    dc2 = G2 % 4
                    hb2 = G2 % 2
                    vector.wait_ge(a_bc.h, 32 * j2 + 8 * sg2 + 8)
                    vector.tensor_copy(
                        out=sent_col(bu4[:, hb2, :], 1).rearrange(
                            "p b c -> p (b c)"),
                        in_=carry[:, dc2, 4 * sg2 + 1:4 * sg2 + 4])
                    for k in range(4):
                        v2 = vector.tensor_tensor(
                            out=bu4[:, hb2, k * TB:k * TB + TC],
                            in0=u_sb[:, dc2, :],
                            in1=brep[:, sg2 % 2, k, :], op=ALU.mult)
                    g_bu.inc(v2)

                emit_bu(16 * j)
                for sg in range(4):
                    for dc in range(4):
                        G = 16 * j + 4 * sg + dc
                        hb = G % 2
                        vector.wait_ge(a_ex.h, G + 1)
                        v = vector.tensor_tensor_scan(
                            out=h4[:, hb, 0:FD4], data0=a4[:, hb, 0:FD4],
                            data1=bu4[:, hb, 0:FD4],
                            initial=carry[:, dc, 4 * sg:4 * sg + 1].opt(),
                            op0=ALU.mult, op1=ALU.add)
                        d_sc.inc(v)
                        if 4 * sg + dc < 15:
                            emit_bu(G + 1)
                        vector.wait_ge(d_sc.h, G + 1)
                        v = vector.tensor_copy(
                            out=carry[:, dc, 4 * sg:4 * sg + 4],
                            in_=carry_cols(hb))
                        d_cy.inc(v)
                        bq = sg % 2
                        for k in range(4):
                            v = vector.tensor_tensor(
                                out=tmp4[:, k, :],
                                in0=h4[:, hb, k * TB:k * TB + TC],
                                in1=crep[:, bq, k, :], op=ALU.mult)
                        d_cm.inc(v)
                        vector.tensor_tensor(
                            out=pair2[:].rearrange("p a b -> p (a b)"),
                            in0=tmp4[:, 0:2, :].rearrange("p a b -> p (a b)"),
                            in1=tmp4[:, 2:4, :].rearrange("p a b -> p (a b)"),
                            op=ALU.add)
                        if sg == 0:
                            vector.tensor_tensor(out=yac_sb[:, dc, :],
                                                 in0=pair2[:, 0, :],
                                                 in1=pair2[:, 1, :],
                                                 op=ALU.add)
                        else:
                            vector.tensor_tensor(out=tmp4[:, 0, :],
                                                 in0=pair2[:, 0, :],
                                                 in1=pair2[:, 1, :],
                                                 op=ALU.add)
                            vector.tensor_add(out=yac_sb[:, dc, :],
                                              in0=yac_sb[:, dc, :],
                                              in1=tmp4[:, 0, :])
                vector.wait_ge(a_spl.h, 4 * (j + 1))
                if j >= 1:
                    vector.wait_ge(g_dma.h, 16 * j)
                for dc in range(4):
                    v = vector.scalar_tensor_tensor(
                        out=t1_sb[:], in0=xc_sb[:, bf, dc, :],
                        scalar=dv_sb[:, dc:dc + 1], in1=yac_sb[:, dc, :],
                        op0=ALU.mult, op1=ALU.add)
                    vector.wait_ge(a_sz.h, 4 * j + dc + 1)
                    v = vector.tensor_tensor(out=gout_sb[:, 0, dc, :],
                                             in0=t1_sb[:], in1=sz_sb[:],
                                             op=ALU.mult)
                    g_gt.inc(v)

        @block.scalar
        def _(scalar):
            scalar.wait_ge(ld_w.h, LDW_TOTAL)
            scalar.copy(out=sel_sb[:, :, :], in_=selstg[:, :, :])
            v = scalar.activation(out=ea_sb[:], in_=ea_sb[:], func=AF.Exp)
            d_pre.inc(v)
            scalar.wait_ge(d_pre.h, 2)
            for j in range(NCH):
                bf = j % 2
                scalar.wait_ge(ld_c.h, 48 * j + 32)
                if j > 1:
                    scalar.wait_ge(p_bc.h, 32 * j)
                v = scalar.copy(out=bcbf_sb[:, bf, :], in_=bc_sb[0:32, bf, :])
                a_bcc.inc(v)
                scalar.wait_ge(g_gt.h, 4 * j)
                for dc in range(4):
                    if 4 * j + dc > 0:
                        scalar.wait_ge(d_dt.h, 4 * j + dc)
                    scalar.wait_ge(p_dt.h, 4 * j + dc + 1)
                    scalar.activation(out=sp_r[:], in_=ps_dt[:], func=AF.Relu,
                                      bias=bdt_sb[:, dc:dc + 1])
                    v = scalar.activation(out=t1_sb[:], in_=ps_dt[:],
                                          func=AF.Abs,
                                          bias=bdt_sb[:, dc:dc + 1])
                    a_spp.inc(v)
                    scalar.activation(out=t_a[:], in_=t1_sb[:], func=AF.Exp,
                                      scale=-1.0)
                    v = scalar.activation(out=t_a[:], in_=t_a[:], func=AF.Ln,
                                          bias=1.0)
                    a_spl.inc(v)
                for sg in range(4):
                    bq = sg % 2
                    for s2 in range(4):
                        prev = 16 * j + 4 * (sg - 1)
                        if prev > 0:
                            scalar.wait_ge(g_bu.h, prev)
                            scalar.wait_ge(d_cm.h, prev)
                        scalar.wait_ge(p_bc.h, 32 * j + 8 * sg + 2 * s2 + 1)
                        v = scalar.copy(out=brep[:, bq, s2, :],
                                        in_=ps_bc[(8 * sg + 2 * s2) % 2][:])
                        a_bc.inc(v)
                        scalar.wait_ge(p_bc.h, 32 * j + 8 * sg + 2 * s2 + 2)
                        v = scalar.copy(out=crep[:, bq, s2, :],
                                        in_=ps_bc[(8 * sg + 2 * s2 + 1) % 2][:])
                        a_bc.inc(v)
                    for dc in range(4):
                        G = 16 * j + 4 * sg + dc
                        hb = G % 2
                        scalar.wait_ge(d_dt.h, 4 * j + 4)
                        if G >= 2:
                            scalar.wait_ge(d_sc.h, G - 1)
                        for s2 in range(4):
                            s = 4 * sg + s2
                            v = scalar.activation(
                                out=a4[:, hb, s2 * TB:s2 * TB + TC],
                                in_=dt_sb[:, dc, :], func=AF.Exp,
                                scale=negA[:, dc, s:s + 1].opt())
                        a_ex.inc(v)
                scalar.wait_ge(ld_c.h, 48 * (j + 1))
                for dc in range(4):
                    scalar.wait_ge(g_gt.h, 4 * j + dc)
                    v = scalar.activation(out=sz_sb[:], in_=z_sb[:, dc, :],
                                          func=AF.Silu)
                    a_sz.inc(v)

        @block.tensor
        def _(tensor):
            for j in range(NCH):
                bf = j % 2
                tensor.wait_ge(ld_c.h, 48 * j + 32)
                for dc in range(4):
                    if 4 * j + dc > 0:
                        tensor.wait_ge(a_spp.h, 4 * j + dc)
                    for hh in range(2):
                        v = tensor.matmul(
                            ps_dt[:, hh * 512:(hh + 1) * 512],
                            wdt_sb[32:33, dc * 128:(dc + 1) * 128],
                            bc_sb[32:33, bf, hh * 512:(hh + 1) * 512],
                            start=True, stop=True)
                    p_dt.inc(v)
                tensor.wait_ge(a_bcc.h, j + 1)
                for sg in range(4):
                    for s2 in range(4):
                        s = 4 * sg + s2
                        for which in range(2):
                            k = 32 * j + 8 * sg + 2 * s2 + which
                            if k >= 2:
                                tensor.wait_ge(a_bc.h, k - 1)
                            row = s if which == 0 else 16 + s
                            for hh in range(2):
                                v = tensor.matmul(
                                    ps_bc[k % 2][:, hh * 512:(hh + 1) * 512],
                                    sel_sb[:, row, :],
                                    bcbf_sb[:, bf, hh * 512:(hh + 1) * 512],
                                    start=True, stop=True)
                            p_bc.inc(v)

        @block.gpsimd
        def _(g):
            for j in range(NCH):
                g.wait_ge(g_gt.h, 4 * (j + 1))
                v = g.dma_start(
                    out=g_d[:, :].rearrange("(c p) t -> p c t", p=128)[
                        :, :, j * TC:(j + 1) * TC],
                    in_=gout_sb[:, 0, :, :])
                g_dma.inc(v, 16)
            g.wait_ge(g_dma.h, g_dma.n)

    return nc


def build_l3():
    nc = bass.Bass("TRN2", target_bir_lowering=False)
    TC = 512
    NCH = TQ // TC
    TH = TQ // 2

    g_in = nc.dram_tensor("gall", [DI, TQ], BF16, kind="ExternalInput")
    xq_d = nc.dram_tensor("xq", [DM, TQ], F32, kind="ExternalInput")
    woT_d = nc.dram_tensor("woT", [DI, DM], F32, kind="ExternalInput")
    w1T_d = nc.dram_tensor("w1T", [DM, 4096], F32, kind="ExternalInput")
    b1_d = nc.dram_tensor("b1", [128, 32], F32, kind="ExternalInput")
    w2T_d = nc.dram_tensor("w2T", [4096, DM], F32, kind="ExternalInput")
    b2_d = nc.dram_tensor("b2", [128, 8], F32, kind="ExternalInput")

    out_d = nc.dram_tensor("out", [DM, TQ], F32, kind="ExternalOutput")
    x1f_d = nc.dram_tensor("x1f", [DM, TQ], F32)
    h1_d = nc.dram_tensor("h1", [4096, TQ], BF16)
    w2b_d = nc.dram_tensor("w2b", [4096, DM], BF16)

    es = ExitStack()
    with es:
        ar = Arena(es, nc, "ar", 49152)
        wo_sb = ar.view(0, [128, 16, DM], BF16)
        wstg = ar.view(8192, [128, 16, 256], F32)
        g_sb = ar.view(12288, [128, 2, 16, TC], BF16)
        x_sb = ar.view(20480, [128, 2, 8, TC], F32)
        mo_sb = ar.view(28672, [128, 2, 8, TC], F32)
        sqA = ar.view(36864, [128, 2, 8, TC], F32)
        x1cA = ar.view(45056, [128, 8, TC], F32)
        x1stg = ar.view(0, [128, 8, TH], F32)
        x1bh = ar.view(16384, [128, 8, TH], BF16)
        w1stg = ar.view(24576, [128, 2, 8, 128], F32)
        w1c = ar.view(26624, [128, 2, 8, 128], BF16)
        h1o = ar.view(27648, [128, 2, 4, TC], BF16)
        w2stg = ar.view(0, [128, 32, 128], F32)
        w2c = ar.view(4096, [128, 2, 32, 128], BF16)
        h1t = ar.view(12288, [128, 32, TC], BF16)
        x1cC = ar.view(20480, [128, 8, TC], F32)
        r2_sb = ar.view(24576, [128, 2, 8, TC], F32)
        sqC = ar.view(32768, [128, 2, 8, TC], F32)
        o_sb = ar.view(40960, [128, 8, TC], F32)
        rows = _sb(es, nc, "rows", [1, 4, TC], F32)
        rep1 = _sb(es, nc, "rep1", [128, TC], F32)
        rep2 = _sb(es, nc, "rep2", [128, TC], F32)
        ones_sb = _sb(es, nc, "ones_sb", [1, 128], F32)
        onesc_sb = _sb(es, nc, "onesc_sb", [128, 1], F32)
        b1_sb = _sb(es, nc, "b1_sb", [128, 32], F32)
        b2_sb = _sb(es, nc, "b2_sb", [128, 8], F32)

        ps_mm = [_ps(es, nc, f"ps_mm{i}", [128, TC]) for i in range(2)]
        ps_s1 = _ps(es, nc, "ps_s1", [1, TC])
        ps_s2 = _ps(es, nc, "ps_s2", [1, TC])
        ps_r1 = _ps(es, nc, "ps_r1", [128, TC])
        ps_r2 = _ps(es, nc, "ps_r2", [128, TC])

        ld_w = Sem(es, nc, "ld_w")
        ld_c = Sem(es, nc, "ld_c")
        a_wc = Sem(es, nc, "a_wc")
        p_mo = Sem(es, nc, "p_mo")
        d_mo = Sem(es, nc, "d_mo")
        a_sq = Sem(es, nc, "a_sq")
        p_st = Sem(es, nc, "p_st")
        d_rw = Sem(es, nc, "d_rw")
        p_rp = Sem(es, nc, "p_rp")
        a_rp = Sem(es, nc, "a_rp")
        d_x1 = Sem(es, nc, "d_x1")
        s_x1 = Sem(es, nc, "s_x1")
        ld_xh = Sem(es, nc, "ld_xh")
        a_xh = Sem(es, nc, "a_xh")
        ld_1 = Sem(es, nc, "ld_1")
        a_1c = Sem(es, nc, "a_1c")
        p_h1 = Sem(es, nc, "p_h1")
        a_h1 = Sem(es, nc, "a_h1")
        s_h1 = Sem(es, nc, "s_h1")
        ld_2s = Sem(es, nc, "ld_2s")
        a_2c = Sem(es, nc, "a_2c")
        s_2b = Sem(es, nc, "s_2b")
        ld_2 = Sem(es, nc, "ld_2")
        p_h2 = Sem(es, nc, "p_h2")
        a_h2 = Sem(es, nc, "a_h2")
        d_r2 = Sem(es, nc, "d_r2")
        a_q2 = Sem(es, nc, "a_q2")
        p_s2 = Sem(es, nc, "p_s2")
        d_w2 = Sem(es, nc, "d_w2")
        p_p2 = Sem(es, nc, "p_p2")
        a_p2 = Sem(es, nc, "a_p2")
        d_o = Sem(es, nc, "d_o")
        s_o = Sem(es, nc, "s_o")

        block = es.enter_context(nc.Block())
        A_X1 = 8 * NCH
        A_SX = 16 * NCH
        B_H1 = 4 * 64
        B_SH = 16 * 64

        @block.sync
        def _(sync):
            for sl in range(4):
                sync.dma_start(
                    out=wstg[:, :, :],
                    in_=woT_d[:, :].rearrange("(c p) o -> p c o", p=128)[
                        :, :, sl * 256:(sl + 1) * 256]).then_inc(ld_w.h, 16)
                ld_w.n += 16
                sync.wait_ge(a_wc.h, sl + 1)
            for src, dst in ((b1_d[:, :], b1_sb[:]), (b2_d[:, :], b2_sb[:])):
                sync.dma_start(out=dst, in_=src).then_inc(ld_w.h, 16)
                ld_w.n += 16
            for j in range(NCH):
                bf = j % 2
                if j >= 2:
                    sync.wait_ge(d_x1.h, 8 * (j - 1))
                sync.dma_start(
                    out=g_sb[:, bf, :, :],
                    in_=g_in[:, :].rearrange("(c p) t -> p c t", p=128)[
                        :, :, j * TC:(j + 1) * TC]).then_inc(ld_c.h, 16)
                sync.dma_start(
                    out=x_sb[:, bf, :, :],
                    in_=xq_d[:, :].rearrange("(c p) t -> p c t", p=128)[
                        :, :, j * TC:(j + 1) * TC]).then_inc(ld_c.h, 16)
                ld_c.n += 32
            sync.wait_ge(d_x1.h, A_X1)
            sync.wait_ge(s_x1.h, A_SX)
            sync.wait_ge(p_rp.h, 2 * NCH)
            for th in range(2):
                sync.dma_start(
                    out=x1stg[:, :, :],
                    in_=x1f_d[:, :].rearrange("(c p) t -> p c t", p=128)[
                        :, :, th * TH:(th + 1) * TH]).then_inc(ld_xh.h, 16)
                ld_xh.n += 16
                sync.wait_ge(a_xh.h, th + 1)
                for ok in range(32):
                    i = 32 * th + ok
                    bf = i % 2
                    if i >= 2:
                        sync.wait_ge(a_1c.h, i - 1)
                    sync.dma_start(
                        out=w1stg[:, bf, :, :],
                        in_=w1T_d[:, :].rearrange("(c p) o -> p c o", p=128)[
                            :, :, ok * 128:(ok + 1) * 128]).then_inc(ld_1.h, 16)
                    ld_1.n += 16
            sync.wait_ge(s_h1.h, B_SH)
            for oc in range(8):
                if oc >= 1:
                    sync.wait_ge(s_2b.h, 16 * oc)
                sync.dma_start(
                    out=w2stg[:, :, :],
                    in_=w2T_d[:, :].rearrange("(c p) o -> p c o", p=128)[
                        :, :, oc * 128:(oc + 1) * 128]).then_inc(ld_2s.h, 16)
                ld_2s.n += 16
            sync.wait_ge(s_2b.h, 8 * 16)
            for j in range(NCH):
                if j >= 1:
                    sync.wait_ge(d_r2.h, 8 * j)
                    sync.wait_ge(p_h2.h, 8 * j)
                sync.dma_start(
                    out=h1t[:, :, :],
                    in_=h1_d[:, :].rearrange("(c p) t -> p c t", p=128)[
                        :, :, j * TC:(j + 1) * TC]).then_inc(ld_2.h, 16)
                sync.dma_start(
                    out=x1cC[:, :, :],
                    in_=x1f_d[:, :].rearrange("(c p) t -> p c t", p=128)[
                        :, :, j * TC:(j + 1) * TC]).then_inc(ld_2.h, 16)
                for oc in range(8):
                    if 8 * j + oc >= 2:
                        sync.wait_ge(p_h2.h, 8 * j + oc - 1)
                    sync.dma_start(
                        out=w2c[:, (8 * j + oc) % 2, :, :],
                        in_=w2b_d[:, :].rearrange("(c p) o -> p c o", p=128)[
                            :, :, oc * 128:(oc + 1) * 128]).then_inc(ld_2.h, 16)
                    ld_2.n += 16
                ld_2.n += 32

        @block.scalar
        def _(scalar):
            for sl in range(4):
                scalar.wait_ge(ld_w.h, 16 * (sl + 1))
                v = scalar.copy(out=wo_sb[:, :, sl * 256:(sl + 1) * 256],
                                in_=wstg[:, :, :])
                a_wc.inc(v)
            for j in range(NCH):
                bf = j % 2
                if j >= 2:
                    scalar.wait_ge(d_x1.h, 8 * (j - 1))
                    scalar.wait_ge(p_st.h, 2 * (j - 1))
                for c in range(8):
                    scalar.wait_ge(d_mo.h, 8 * j + c + 1)
                    v = scalar.activation(out=sqA[:, bf, c, :],
                                          in_=mo_sb[:, bf, c, :],
                                          func=AF.Square)
                a_sq.inc(v)
                scalar.wait_ge(d_rw.h, 3 * j + 1)
                scalar.activation(out=rows[:, 3, :], in_=rows[:, 2, :],
                                  func=AF.Ln)
                v = scalar.activation(out=rows[:, 2, :], in_=rows[:, 3, :],
                                      func=AF.Exp, scale=-0.5)
                d_rw.inc(v)
                scalar.wait_ge(p_rp.h, 2 * (j + 1))
                v = scalar.copy(out=rep1[:], in_=ps_r1[:])
                a_rp.inc(v)
                v = scalar.copy(out=rep2[:], in_=ps_r2[:])
                a_rp.inc(v)
            for th in range(2):
                scalar.wait_ge(ld_xh.h, 16 * (th + 1))
                if th > 0:
                    scalar.wait_ge(p_h1.h, 4 * 32)
                for c in range(8):
                    v = scalar.copy(out=x1bh[:, c, :], in_=x1stg[:, c, :])
                a_xh.inc(v)
                for ok in range(32):
                    i = 32 * th + ok
                    bf = i % 2
                    scalar.wait_ge(ld_1.h, 16 * (i + 1))
                    if i >= 2:
                        scalar.wait_ge(p_h1.h, 4 * (i - 1))
                    v = scalar.copy(out=w1c[:, bf, :, :],
                                    in_=w1stg[:, bf, :, :])
                    a_1c.inc(v)

            for oc in range(8):
                scalar.wait_ge(ld_2s.h, 16 * (oc + 1))
                if oc >= 1:
                    scalar.wait_ge(s_2b.h, 16 * oc)
                v = scalar.copy(out=w2c[:, 0, :, :], in_=w2stg[:, :, :])
                a_2c.inc(v)
            for j in range(NCH):
                bf = j % 2
                if j >= 2:
                    scalar.wait_ge(d_o.h, 8 * (j - 1))
                    scalar.wait_ge(p_s2.h, 2 * (j - 1))
                for c in range(8):
                    scalar.wait_ge(d_r2.h, 8 * j + c + 1)
                    v = scalar.activation(out=sqC[:, bf, c, :],
                                          in_=r2_sb[:, bf, c, :],
                                          func=AF.Square)
                a_q2.inc(v)
                scalar.wait_ge(d_w2.h, 3 * j + 1)
                scalar.activation(out=rows[:, 3, :], in_=rows[:, 2, :],
                                  func=AF.Ln)
                v = scalar.activation(out=rows[:, 2, :], in_=rows[:, 3, :],
                                      func=AF.Exp, scale=-0.5)
                d_w2.inc(v)
                scalar.wait_ge(p_p2.h, 2 * (j + 1))
                v = scalar.copy(out=rep1[:], in_=ps_r1[:])
                a_p2.inc(v)
                v = scalar.copy(out=rep2[:], in_=ps_r2[:])
                a_p2.inc(v)

        @block.vector
        def _(vector):
            vector.memset(ones_sb[:], 1.0)
            vector.memset(onesc_sb[:], 1.0)
            for j in range(NCH):
                bf = j % 2
                vector.wait_ge(ld_c.h, 32 * (j + 1))
                if j >= 2:
                    vector.wait_ge(p_st.h, 2 * (j - 1))
                    vector.wait_ge(d_x1.h, 8 * (j - 1))
                for oc in range(8):
                    vector.wait_ge(p_mo.h, 8 * j + oc + 1)
                    v = vector.scalar_tensor_tensor(
                        out=mo_sb[:, bf, oc, :], in0=x_sb[:, bf, oc, :],
                        scalar=2.0, in1=ps_mm[oc % 2][:],
                        op0=ALU.mult, op1=ALU.add)
                    d_mo.inc(v)
                vector.wait_ge(p_st.h, 2 * (j + 1))
                vector.tensor_scalar(out=rows[:, 0, :], in0=ps_s1[:],
                                     scalar1=1.0 / DM, scalar2=None,
                                     op0=ALU.mult)
                vector.tensor_tensor(out=rows[:, 1, :], in0=rows[:, 0, :],
                                     in1=rows[:, 0, :], op=ALU.mult)
                vector.tensor_scalar_add(out=rows[:, 1, :],
                                         in0=rows[:, 1, :],
                                         scalar1=-EPS_LN)
                v = vector.scalar_tensor_tensor(out=rows[:, 2, :],
                                                in0=ps_s2[:], scalar=1.0 / DM,
                                                in1=rows[:, 1, :],
                                                op0=ALU.mult, op1=ALU.subtract)
                d_rw.inc(v)
                vector.wait_ge(d_rw.h, 3 * j + 2)
                v = vector.tensor_tensor(out=rows[:, 1, :], in0=rows[:, 0, :],
                                         in1=rows[:, 2, :], op=ALU.mult)
                d_rw.inc(v)
                vector.wait_ge(a_rp.h, 2 * (j + 1))
                if j > 0:
                    vector.wait_ge(s_x1.h, 16 * j)
                for oc in range(8):
                    vector.tensor_tensor(out=sqA[:, bf, oc, :],
                                         in0=mo_sb[:, bf, oc, :],
                                         in1=rep1[:], op=ALU.mult)
                    v = vector.tensor_tensor(out=x1cA[:, oc, :],
                                             in0=sqA[:, bf, oc, :],
                                             in1=rep2[:], op=ALU.subtract)
                    d_x1.inc(v)
            for th in range(2):
                for ok in range(32):
                    i = 32 * th + ok
                    bf = i % 2
                    if i >= 2:
                        vector.wait_ge(s_h1.h, 16 * (i - 1))
                    for tch in range(4):
                        vector.wait_ge(p_h1.h, 4 * i + tch + 1)
                        v = vector.tensor_scalar(
                            out=h1o[:, bf, tch, :], in0=ps_mm[tch % 2][:],
                            scalar1=b1_sb[:, ok:ok + 1], scalar2=None,
                            op0=ALU.add)
                        a_h1.inc(v)
            for j in range(NCH):
                bf = j % 2
                vector.wait_ge(ld_2.h, 160 * j + 32)
                if j >= 2:
                    vector.wait_ge(p_s2.h, 2 * (j - 1))
                    vector.wait_ge(d_o.h, 8 * (j - 1))
                for oc in range(8):
                    vector.wait_ge(p_h2.h, 8 * j + oc + 1)
                    v = vector.tensor_scalar(
                        out=r2_sb[:, bf, oc, :], in0=ps_mm[oc % 2][:],
                        scalar1=b2_sb[:, oc:oc + 1], scalar2=None,
                        op0=ALU.add)
                    a_h2.inc(v)
                    v = vector.tensor_add(out=r2_sb[:, bf, oc, :],
                                          in0=r2_sb[:, bf, oc, :],
                                          in1=x1cC[:, oc, :])
                    d_r2.inc(v)
                vector.wait_ge(p_s2.h, 2 * (j + 1))
                vector.tensor_scalar(out=rows[:, 0, :], in0=ps_s1[:],
                                     scalar1=1.0 / DM, scalar2=None,
                                     op0=ALU.mult)
                vector.tensor_tensor(out=rows[:, 1, :], in0=rows[:, 0, :],
                                     in1=rows[:, 0, :], op=ALU.mult)
                vector.tensor_scalar_add(out=rows[:, 1, :],
                                         in0=rows[:, 1, :],
                                         scalar1=-EPS_LN)
                v = vector.scalar_tensor_tensor(out=rows[:, 2, :],
                                                in0=ps_s2[:], scalar=1.0 / DM,
                                                in1=rows[:, 1, :],
                                                op0=ALU.mult, op1=ALU.subtract)
                d_w2.inc(v)
                vector.wait_ge(d_w2.h, 3 * j + 2)
                v = vector.tensor_tensor(out=rows[:, 1, :], in0=rows[:, 0, :],
                                         in1=rows[:, 2, :], op=ALU.mult)
                d_w2.inc(v)
                vector.wait_ge(a_p2.h, 2 * (j + 1))
                if j > 0:
                    vector.wait_ge(s_o.h, 16 * j)
                for oc in range(8):
                    vector.tensor_tensor(out=sqC[:, bf, oc, :],
                                         in0=r2_sb[:, bf, oc, :],
                                         in1=rep1[:], op=ALU.mult)
                    v = vector.tensor_tensor(out=o_sb[:, oc, :],
                                             in0=sqC[:, bf, oc, :],
                                             in1=rep2[:], op=ALU.subtract)
                    d_o.inc(v)

        @block.tensor
        def _(tensor):
            def stats_a(j):
                bfj = j % 2
                tensor.wait_ge(a_sq.h, j + 1)
                if j > 0:
                    tensor.wait_ge(d_rw.h, 3 * (j - 1) + 1)
                for c in range(8):
                    v = tensor.matmul(ps_s1[:], onesc_sb[:],
                                      mo_sb[:, bfj, c, :],
                                      start=(c == 0), stop=(c == 7))
                p_st.inc(v)
                for c in range(8):
                    v = tensor.matmul(ps_s2[:], onesc_sb[:],
                                      sqA[:, bfj, c, :],
                                      start=(c == 0), stop=(c == 7))
                p_st.inc(v)
                tensor.wait_ge(d_rw.h, 3 * (j + 1))
                if j > 0:
                    tensor.wait_ge(a_rp.h, 2 * j)
                v = tensor.matmul(ps_r1[:], ones_sb[:], rows[:, 2, :],
                                  start=True, stop=True)
                p_rp.inc(v)
                v = tensor.matmul(ps_r2[:], ones_sb[:], rows[:, 1, :],
                                  start=True, stop=True)
                p_rp.inc(v)

            for j in range(NCH):
                bf = j % 2
                tensor.wait_ge(ld_c.h, 32 * (j + 1))
                tensor.wait_ge(a_wc.h, 4)
                for oc in range(8):
                    if 8 * j + oc >= 2:
                        tensor.wait_ge(d_mo.h, 8 * j + oc - 1)
                    for k in range(16):
                        v = tensor.matmul(ps_mm[oc % 2][:],
                                          wo_sb[:, k, oc * 128:(oc + 1) * 128],
                                          g_sb[:, bf, k, :],
                                          start=(k == 0), stop=(k == 15))
                    p_mo.inc(v)
                stats_a(j)
            tensor.wait_ge(d_x1.h, A_X1)
            for th in range(2):
                tensor.wait_ge(a_xh.h, th + 1)
                for ok in range(32):
                    i = 32 * th + ok
                    bf = i % 2
                    tensor.wait_ge(a_1c.h, i + 1)
                    for tch in range(4):
                        if 4 * i + tch >= 2:
                            tensor.wait_ge(a_h1.h, 4 * i + tch - 1)
                        for k in range(8):
                            v = tensor.matmul(
                                ps_mm[tch % 2][:], w1c[:, bf, k, :],
                                x1bh[:, k, tch * TC:(tch + 1) * TC],
                                start=(k == 0), stop=(k == 7))
                        p_h1.inc(v)
            def stats_c(j):
                bfj = j % 2
                tensor.wait_ge(a_q2.h, j + 1)
                if j > 0:
                    tensor.wait_ge(d_w2.h, 3 * (j - 1) + 1)
                for c in range(8):
                    v = tensor.matmul(ps_s1[:], onesc_sb[:],
                                      r2_sb[:, bfj, c, :],
                                      start=(c == 0), stop=(c == 7))
                p_s2.inc(v)
                for c in range(8):
                    v = tensor.matmul(ps_s2[:], onesc_sb[:],
                                      sqC[:, bfj, c, :],
                                      start=(c == 0), stop=(c == 7))
                p_s2.inc(v)
                tensor.wait_ge(d_w2.h, 3 * (j + 1))
                if j > 0:
                    tensor.wait_ge(a_p2.h, 2 * j)
                v = tensor.matmul(ps_r1[:], ones_sb[:], rows[:, 2, :],
                                  start=True, stop=True)
                p_p2.inc(v)
                v = tensor.matmul(ps_r2[:], ones_sb[:], rows[:, 1, :],
                                  start=True, stop=True)
                p_p2.inc(v)

            for j in range(NCH):
                for oc in range(8):
                    tensor.wait_ge(ld_2.h, 160 * j + 32 + 16 * (oc + 1))
                    if 8 * j + oc >= 2:
                        tensor.wait_ge(a_h2.h, 8 * j + oc - 1)
                    for k in range(32):
                        v = tensor.matmul(ps_mm[oc % 2][:],
                                          w2c[:, (8 * j + oc) % 2, k, :],
                                          h1t[:, k, :],
                                          start=(k == 0), stop=(k == 31))
                    p_h2.inc(v)
                stats_c(j)

        @block.gpsimd
        def _(g):
            for j in range(NCH):
                g.wait_ge(d_x1.h, 8 * (j + 1))
                v = g.dma_start(
                    out=x1f_d[:, :].rearrange("(c p) t -> p c t", p=128)[
                        :, :, j * TC:(j + 1) * TC],
                    in_=x1cA[:, :, :])
                s_x1.inc(v, 16)
            for th in range(2):
                for ok in range(32):
                    i = 32 * th + ok
                    bf = i % 2
                    g.wait_ge(a_h1.h, 4 * (i + 1))
                    v = g.dma_start(
                        out=h1_d[ok * 128:(ok + 1) * 128,
                                 th * TH:(th + 1) * TH],
                        in_=h1o[:, bf, :, :].rearrange("p a b -> p (a b)"))
                    s_h1.inc(v, 16)
            for oc in range(8):
                g.wait_ge(a_2c.h, oc + 1)
                v = g.dma_start(
                    out=w2b_d[:, :].rearrange("(c p) o -> p c o", p=128)[
                        :, :, oc * 128:(oc + 1) * 128],
                    in_=w2c[:, 0, :, :])
                s_2b.inc(v, 16)
            for j in range(NCH):
                g.wait_ge(d_o.h, 8 * (j + 1))
                v = g.dma_start(
                    out=out_d[:, :].rearrange("(c p) t -> p c t", p=128)[
                        :, :, j * TC:(j + 1) * TC],
                    in_=o_sb[:, :, :])
                s_o.inc(v, 16)
            g.wait_ge(s_o.h, s_o.n)

    return nc


def _t(a):
    return np.ascontiguousarray(np.asarray(a).T)


def kernel(L0_attn_x, L0_attn_g, L0_attn_W_in, L0_attn_W_out, L0_attn_conv_w,
           L0_attn_conv_b, L0_attn_W_x, L0_attn_W_dt, L0_attn_b_dt,
           L0_attn_A_log, L0_attn_D, L0_ffn_W1, L0_ffn_b1, L0_ffn_W2,
           L0_ffn_b2):
    x = np.asarray(L0_attn_x, np.float32)
    x_fm = [np.ascontiguousarray(x[b].T) for b in range(B)]

    winT = _t(L0_attn_W_in)
    wxT = _t(L0_attn_W_x)
    gvec = np.ascontiguousarray(
        np.asarray(L0_attn_g, np.float32).reshape(8, 128).T)
    conv_w = np.asarray(L0_attn_conv_w, np.float32)
    cdiag = np.zeros((128, 64, 128), np.float32)
    for dc in range(16):
        for k in range(4):
            np.fill_diagonal(cdiag[:, 4 * dc + k, :],
                             conv_w[dc * 128:(dc + 1) * 128, k])
    cdiag = np.ascontiguousarray(cdiag.reshape(128, 64 * 128))
    cb = np.ascontiguousarray(
        np.asarray(L0_attn_conv_b, np.float32).reshape(16, 128).T)
    ident = np.eye(128, dtype=np.float32)

    nc1 = build_l1()
    in_maps1 = []
    zero3 = np.zeros((DM, 3), np.float32)
    for c in range(N_CORES):
        b, q = divmod(c, 4)
        xh = (zero3 if q == 0 else
              np.ascontiguousarray(x_fm[b][:, q * TQ - 3:q * TQ]))
        in_maps1.append({
            "xq": np.ascontiguousarray(x_fm[b][:, q * TQ:(q + 1) * TQ]),
            "xh": xh,
            "winT": winT, "wxT": wxT, "gvec": gvec, "cdiag": cdiag,
            "cb": cb, "ident": ident,
        })
    r1 = run_bass_kernel_spmd(nc1, in_maps1, core_ids=list(range(N_CORES)))
    xc_all = [np.concatenate([r1.results[4 * b + q]["xc"] for q in range(4)],
                             axis=1) for b in range(B)]
    z_all = [np.concatenate([r1.results[4 * b + q]["z"] for q in range(4)],
                            axis=1) for b in range(B)]
    par_all = [np.concatenate([r1.results[4 * b + q]["par"] for q in range(4)],
                              axis=1) for b in range(B)]

    A_log = np.asarray(L0_attn_A_log, np.float32)
    W_dt = np.asarray(L0_attn_W_dt, np.float32)
    b_dt = np.asarray(L0_attn_b_dt, np.float32)
    Dv = np.asarray(L0_attn_D, np.float32)
    sel = np.zeros((32, 32, 128), np.float32)
    for r in range(32):
        sel[r, r, :] = 1.0
    sel = np.ascontiguousarray(sel.reshape(32, 32 * 128))
    nc2 = build_l2()
    in_maps2 = []
    for c in range(N_CORES):
        b, jg = divmod(c, 4)
        sl = slice(jg * CG, (jg + 1) * CG)
        in_maps2.append({
            "xc": np.ascontiguousarray(xc_all[b][sl]),
            "z": np.ascontiguousarray(z_all[b][sl]),
            "par": par_all[b],
            "alog": np.ascontiguousarray(
                A_log[sl].reshape(4, 128, DS).transpose(1, 0, 2)
                .reshape(128, 4 * DS)),
            "wdt": np.ascontiguousarray(W_dt[sl, 0].reshape(1, CG)),
            "bdt": np.ascontiguousarray(b_dt[sl].reshape(4, 128).T),
            "dvec": np.ascontiguousarray(Dv[sl].reshape(4, 128).T),
            "sel": sel,
        })
    r2 = run_bass_kernel_spmd(nc2, in_maps2, core_ids=list(range(N_CORES)))
    g_all = [np.concatenate([r2.results[4 * b + jg]["gact"] for jg in range(4)],
                            axis=0) for b in range(B)]

    woT = _t(L0_attn_W_out)
    W1 = np.asarray(L0_ffn_W1, np.float32)
    b1 = np.asarray(L0_ffn_b1, np.float32)
    W2 = np.asarray(L0_ffn_W2, np.float32)
    b2 = np.asarray(L0_ffn_b2, np.float32)
    nc3 = build_l3()
    in_maps3 = []
    for c in range(N_CORES):
        b, q = divmod(c, 4)
        grp = 4 * b + q
        in_maps3.append({
            "gall": np.ascontiguousarray(g_all[b][:, q * TQ:(q + 1) * TQ]),
            "xq": np.ascontiguousarray(x_fm[b][:, q * TQ:(q + 1) * TQ]),
            "woT": woT,
            "w1T": np.ascontiguousarray(W1[grp].T),
            "b1": np.ascontiguousarray(b1[grp].reshape(32, 128).T),
            "w2T": np.ascontiguousarray(W2[grp].T),
            "b2": np.ascontiguousarray(b2[grp].reshape(8, 128).T),
        })
    r3 = run_bass_kernel_spmd(nc3, in_maps3, core_ids=list(range(N_CORES)))
    out = np.empty((B, T, DM), np.float32)
    for c in range(N_CORES):
        b, q = divmod(c, 4)
        out[b, q * TQ:(q + 1) * TQ, :] = r3.results[c]["out"].T
    return out
